# version 20
# baseline (speedup 1.0000x reference)
import numpy as np
import ml_dtypes
from contextlib import ExitStack

import concourse.bass as bass
import concourse.mybir as mybir

F32 = mybir.dt.float32
BF16 = mybir.dt.bfloat16
AF = mybir.ActivationFunctionType
ALU = mybir.AluOpType
AX = mybir.AxisListType

EPS = 1e-5
T = 2048
D = 2048
NH = 8192
HD = 128
NHEADS = 16
NG = 4
GH = 4
NTT = 16
NTC = 4
NDC = 16
NHB = 64
RSQ = float(1.0 / np.sqrt(NH))

ENGINES = ("sync", "gpsimd", "tensor", "vector", "scalar")


class Sem:
    __slots__ = ("h", "n", "name", "is_dma")

    def __init__(self, ctx, nc, name, is_dma=False):
        self.h = ctx.enter_context(nc.semaphore(name))
        self.n = 0
        self.name = name
        self.is_dma = is_dma


class Em:

    def __init__(self):
        self.q = {e: [] for e in ENGINES}
        self.own = {}

    def wait(self, eng, sem, val):
        if sem.is_dma:
            val = sem.n
        if val > 0:
            q = self.q[eng]
            if q and q[-1][0] == "w" and q[-1][1] is sem:
                if q[-1][2] >= val:
                    return
                q.pop()
            q.append(("w", sem, val))

    def waitsnap(self, eng, snapshot):
        for sem, val in snapshot:
            self.wait(eng, sem, val)

    def op(self, eng, fn, inc=None, k=1):
        own = self.own.get(eng)
        if own is not None:
            self.wait(eng, own, own.n)
            if inc is None:
                inc = own
        if inc is None:
            self.q[eng].append(("o", fn, None, 0))
            return None
        inc.n += k
        self.q[eng].append(("o", fn, inc, k))
        return inc.n

    def emit(self, block):
        for ename in ENGINES:
            items = self.q[ename]

            def run(eng, items=items):
                for it in items:
                    if it[0] == "w":
                        eng.wait_ge(it[1].h, it[2])
                    else:
                        instr = it[1](eng)
                        if it[2] is not None:
                            instr.then_inc(it[2].h, it[3])
            getattr(block, ename)(run)


def build_program(stage="full", dbg=False):
    nc = bass.Bass("TRN2", target_bir_lowering=False, debug=False)
    scr_kind = "ExternalOutput" if dbg else "Internal"

    dr = lambda name, shape, dt, kind: nc.dram_tensor(name, shape, dt, kind=kind).ap()
    x_d = dr("x", [T, D], F32, "ExternalInput")
    wq_d = dr("wq", [16, 128, 2048], F32, "ExternalInput")
    wk_d = dr("wk", [16, 128, 2048], F32, "ExternalInput")
    wv_d = dr("wv", [4, 128, 8192], F32, "ExternalInput")
    wo_d = dr("wo", [2048, 2048], F32, "ExternalInput")
    w1_d = dr("w1", [64, 128, 2048], F32, "ExternalInput")
    w3_d = dr("w3", [64, 128, 2048], F32, "ExternalInput")
    w2p_d = dr("w2p", [8192, 2048], F32, "ExternalInput")
    gamma_d = dr("gamma", [128, 1], F32, "ExternalInput")
    beta_d = dr("beta", [128, 1], F32, "ExternalInput")
    ident_d = dr("ident", [128, 128], BF16, "ExternalInput")
    h128_d = dr("h128", [128, 128], BF16, "ExternalInput")
    h64d_d = dr("h64d", [128, 128], BF16, "ExternalInput")
    oneD_d = dr("oneD", [128, 1], BF16, "ExternalInput")
    onedm_d = dr("onedm", [128, 128], BF16, "ExternalInput")
    dmask_d = dr("dmask", [128, 4 * 512], BF16, "ExternalInput")
    onesr_d = dr("onesr", [1, 128], BF16, "ExternalInput")
    one11_d = dr("one11", [1, 1], BF16, "ExternalInput")
    ecol_d = dr("ecol", [1, 129], BF16, "ExternalInput")

    out_d = dr("out", [T, D], F32, "ExternalOutput")
    u_scr = dr("u_scr", [NH, T], BF16, scr_kind)
    w2pp_scr = dr("w2pp_scr", [NH, D], BF16, scr_kind)
    oT_scr = dr("oT_scr", [D, T], BF16, scr_kind)
    acc_scr = dr("acc_scr", [T, D], F32, scr_kind)
    dbg_d = {}
    if dbg:
        dbg_d["xnT"] = dr("dbg_xnT", [128, NDC * T], BF16, "ExternalOutput")
        dbg_d["ybq"] = dr("dbg_ybq", [128, GH * T], BF16, "ExternalOutput")
        dbg_d["ybk"] = dr("dbg_ybk", [128, GH * T], BF16, "ExternalOutput")
        dbg_d["vaug"] = dr("dbg_vaug", [128, NTT * GH * 129], BF16, "ExternalOutput")
        dbg_d["invk"] = dr("dbg_invk", [128, 64], F32, "ExternalOutput")
        dbg_d["varq"] = dr("dbg_varq", [1, GH * T], BF16, "ExternalOutput")
        dbg_d["oTg"] = dr("dbg_oTg", [128, T], BF16, "ExternalOutput")

    ctx = ExitStack()
    sb = lambda name, shape, dt: ctx.enter_context(nc.sbuf_tensor(name, shape, dt))

    xnT = sb("xnT", [128, NDC * T], BF16)
    xnloT = sb("xnloT", [128, NDC * 128], BF16)
    wslab = [sb(f"wslab_{i}", [128, 2048], BF16) for i in range(3)]
    vslab = sb("vslab", [128, 4096], BF16)
    ident = sb("ident_s", [128, 128], BF16)
    h128 = sb("h128_s", [128, 128], BF16)
    h64d = sb("h64d_s", [128, 128], BF16)
    oneD = sb("oneD_s", [128, 1], BF16)
    onedm = sb("onedm_s", [128, 128], BF16)
    onesr = sb("onesr_s", [1, 128], BF16)
    one11 = sb("one11_s", [1, 1], BF16)
    ecol = sb("ecol_s", [1, 129], BF16)
    gam = sb("gam_s", [128, 1], F32)
    bet = sb("bet_s", [128, 1], F32)
    betc = sb("betc_s", [128, 1], F32)
    betbf = sb("betbf_s", [128, 1], BF16)
    meanb = sb("meanb_s", [1, 1], BF16)

    em = Em()
    ALL = []

    def mksem(name, is_dma=False):
        s_ = Sem(ctx, nc, name, is_dma)
        ALL.append(s_)
        return s_

    s_pe = mksem("s_pe")
    s_dve = mksem("s_dve")
    s_act = mksem("s_act")
    em.own["vector"] = s_dve
    em.own["scalar"] = s_act

    def snap():
        return [(s, s.n) for s in ALL]

    def barrier():
        sn = snap()
        for e in ENGINES:
            em.waitsnap(e, sn)

    s_const = mksem("s_const", True)
    for dst, src in ((ident, ident_d), (h128, h128_d), (h64d, h64d_d),
                     (oneD, oneD_d), (onedm, onedm_d), (onesr, onesr_d),
                     (one11, one11_d), (ecol, ecol_d), (gam, gamma_d), (bet, beta_d)):
        em.op("sync", lambda e, d=dst, s=src: e.dma_start(out=d[:], in_=s), s_const, 16)
    barrier()

    tok = {}

    def gtok(name):
        return tok.get(name, (s_pe, 0))

    lnc = ExitStack()
    lsb = lambda name, shape, dt: lnc.enter_context(nc.sbuf_tensor(name, shape, dt))
    lps = lambda name, shape: lnc.enter_context(nc.psum_tensor(name, shape, F32))
    xt32 = [lsb(f"xt32_{i}", [128, 2048], F32) for i in range(2)]
    xsqt = lsb("xsqt", [128, 2048], F32)
    xn32t = lsb("xn32t", [128, 2048], F32)
    xnlon = lsb("xnlon", [128, 2048], BF16)
    xnbf = [lsb(f"xnbf_{i}", [128, 2048], BF16) for i in range(2)]
    st_c = {n: [lsb(f"ln_{n}_{i}", [128, 1], F32) for i in range(2)]
            for n in ("xsum", "xss", "mu", "var", "rstd", "nb")}
    ps_tr = [lps(f"ps_tr_{i}", [128, 512]) for i in range(2)]
    s_xt = [mksem(f"s_xt{i}", True) for i in range(2)]

    dv = em.op("vector", lambda e: e.tensor_copy(betbf[:], bet[:]), s_dve)
    em.wait("tensor", s_dve, dv)
    pv = em.op("tensor", lambda e: e.matmul(
        ps_tr[0][0:1, 0:1], oneD[:], betbf[:], start=True, stop=True), s_pe)
    em.wait("scalar", s_pe, pv)
    av = em.op("scalar", lambda e: e.activation(
        meanb[:], ps_tr[0][0:1, 0:1], AF.Copy), s_act)
    em.wait("tensor", s_act, av)
    pv = em.op("tensor", lambda e: e.matmul(
        ps_tr[0][:, 0:1], onesr[:], meanb[:], start=True, stop=True), s_pe)
    em.wait("vector", s_pe, pv)
    dv = em.op("vector", lambda e: e.tensor_sub(betc[:], bet[:], ps_tr[0][:, 0:1]), s_dve)

    tok_xt = [(s_dve, 0), (s_dve, 0)]
    tok_xn = [(s_pe, 0), (s_pe, 0)]
    tok_pstr = [(s_dve, dv), (s_dve, 0)]

    w2pb = [lsb(f"w2pb_{i}", [128, 2048], BF16) for i in range(4)]
    w2pb32 = [lsb(f"w2pb32_{i}", [128, 2048], F32) for i in range(2)]
    foldst = [lsb(f"foldst_{i}", [128, 2048], BF16) for i in range(2)]
    ps_f = [lps(f"ps_f_{i}", [128, 512]) for i in range(2)]
    s_w2p = [mksem(f"s_w2p{i}", True) for i in range(4)]
    s_w2p32 = [mksem(f"s_w2p32_{i}", True) for i in range(2)]
    s_fst = [mksem(f"s_fst{i}", True) for i in range(2)]
    w2pp_view = w2pp_scr.rearrange("(a bb) d -> bb a d", bb=128)

    def emit_w2pload(blk_i):
        bb = blk_i % 4
        if blk_i % 2 == 0:
            em.wait("gpsimd", *gtok(f"w2pb{bb}"))
            em.op("gpsimd", lambda e, bb=bb, blk_i=blk_i: e.dma_start(
                out=w2pb[bb][:], in_=w2p_d[blk_i * 128:(blk_i + 1) * 128, :]),
                s_w2p[bb], 16)
        else:
            st = (blk_i % 4) // 2
            em.wait("sync", *gtok(f"w2pb32_{st}"))
            em.op("sync", lambda e, st=st, blk_i=blk_i: e.dma_start(
                out=w2pb32[st][:], in_=w2p_d[blk_i * 128:(blk_i + 1) * 128, :]),
                s_w2p32[st], 16)
            em.wait("vector", s_w2p32[st], 0)
            em.wait("vector", *gtok(f"w2pb{bb}"))
            dv = em.op("vector", lambda e, bb=bb, st=st: e.tensor_copy(
                w2pb[bb][:], w2pb32[st][:]), s_dve)
            tok[f"w2pb32_{st}"] = (s_dve, dv)
            tok[f"w2pb{bb}_rdy"] = (s_dve, dv)

    def emit_fold_job(blk_i):
        bb = blk_i % 4
        fs = blk_i % 2
        if blk_i + 3 < NHB:
            emit_w2pload(blk_i + 3)
        dv_last = None
        pv_last = None
        for ddc in range(4):
            pb = ddc % 2
            if blk_i % 2 == 0:
                em.wait("tensor", s_w2p[bb], 0)
            else:
                em.wait("tensor", *gtok(f"w2pb{bb}_rdy"))
            em.wait("tensor", *gtok(f"ps_f{pb}"))
            pv = em.op("tensor", lambda e, pb=pb, bb=bb, ddc=ddc: e.matmul(
                ps_f[pb][:], h64d[:], w2pb[bb][:, ddc * 512:(ddc + 1) * 512],
                start=True, stop=True), s_pe)
            pv_last = pv
            em.wait("vector", s_pe, pv)
            if ddc == 0:
                em.wait("vector", *gtok(f"foldst{fs}"))
            dv = em.op("vector", lambda e, pb=pb, fs=fs, ddc=ddc: e.tensor_scalar_mul(
                foldst[fs][:, ddc * 512:(ddc + 1) * 512], ps_f[pb][:], RSQ), s_dve)
            tok[f"ps_f{pb}"] = (s_dve, dv)
            dv_last = dv
        tok[f"w2pb{bb}"] = (s_pe, pv_last)
        q = "scalar"
        em.wait(q, s_dve, dv_last)
        for bit in range(2):
            em.op(q, lambda e, blk_i=blk_i, bit=bit, fs=fs: e.dma_start(
                out=w2pp_view[2 * blk_i + bit],
                in_=foldst[fs][bit * 64:(bit + 1) * 64, :]), s_fst[fs], 16)
        tok[f"foldst{fs}"] = (s_fst[fs], 0)

    emit_w2pload(0)
    emit_w2pload(1)
    emit_w2pload(2)

    def emit_xload(j):
        jb = j % 2
        em.wait("scalar", *tok_xt[jb])
        em.op("scalar", lambda e, j=j, jb=jb: e.dma_start(
            out=xt32[jb][:], in_=x_d[j * 128:(j + 1) * 128, :]), s_xt[jb], 16)

    emit_xload(0)
    for j in range(NTT):
        if j + 1 < NTT:
            emit_xload(j + 1)
        for _fj in range(4):
            emit_fold_job(j * 4 + _fj)
        jb = j % 2
        em.wait("vector", s_xt[jb], 0)
        rv = em.op("vector", lambda e, jb=jb: e.reduce_sum(
            st_c["xsum"][jb][:], xt32[jb][:], axis=AX.X), s_dve)
        em.wait("scalar", s_xt[jb], 0)
        av = em.op("scalar", lambda e, jb=jb: e.activation(
            xsqt[:], xt32[jb][:], AF.Square, accum_out=st_c["xss"][jb][:]), s_act)
        em.wait("vector", s_act, av)
        em.wait("vector", s_dve, rv)
        em.op("vector", lambda e, jb=jb: e.tensor_scalar_mul(
            st_c["mu"][jb][:], st_c["xsum"][jb][:], 1.0 / 2048.0))
        em.op("vector", lambda e, jb=jb: e.tensor_scalar_mul(
            st_c["var"][jb][:], st_c["xss"][jb][:], 1.0 / 2048.0))
        dv = em.op("vector", lambda e, jb=jb: e.scalar_tensor_tensor(
            st_c["rstd"][jb][:], st_c["mu"][jb][:], -1.0, st_c["mu"][jb][:],
            ALU.mult, ALU.mult), s_dve)
        em.wait("vector", s_dve, dv)
        dv = em.op("vector", lambda e, jb=jb: e.tensor_add(
            st_c["var"][jb][:], st_c["var"][jb][:], st_c["rstd"][jb][:]), s_dve)
        em.wait("vector", s_dve, dv)
        dv = em.op("vector", lambda e, jb=jb: e.tensor_scalar_add(
            st_c["var"][jb][:], st_c["var"][jb][:], EPS), s_dve)
        em.wait("scalar", s_dve, dv)
        av = em.op("scalar", lambda e, jb=jb: e.activation(
            st_c["rstd"][jb][:], st_c["var"][jb][:], AF.Ln), s_act)
        em.wait("scalar", s_act, av)
        av = em.op("scalar", lambda e, jb=jb: e.activation(
            st_c["rstd"][jb][:], st_c["rstd"][jb][:], AF.Exp, scale=-0.5), s_act)
        em.wait("vector", s_act, av)
        dv = em.op("vector", lambda e, jb=jb: e.scalar_tensor_tensor(
            st_c["nb"][jb][:], st_c["mu"][jb][:], -1.0, st_c["rstd"][jb][:],
            ALU.mult, ALU.mult), s_dve)
        em.wait("vector", s_dve, dv)
        em.wait("vector", *tok_xn[jb])
        av = em.op("vector", lambda e, jb=jb: e.tensor_scalar(
            xnbf[jb][:], xt32[jb][:], st_c["rstd"][jb][:], st_c["nb"][jb][:],
            ALU.mult, ALU.add), s_dve)
        if j == 0:
            av = em.op("vector", lambda e, jb=jb: e.tensor_scalar(
                xn32t[:], xt32[jb][:], st_c["rstd"][jb][:], st_c["nb"][jb][:],
                ALU.mult, ALU.add), s_dve)
            em.wait("vector", s_dve, av)
            av = em.op("vector", lambda e, jb=jb: e.tensor_sub(
                xnlon[:], xn32t[:], xnbf[jb][:]), s_dve)
        tok_xt[jb] = (s_dve, av)
        for dq in range(4):
            qb = dq % 2
            em.wait("tensor", s_dve, av)
            em.wait("tensor", *tok_pstr[qb])
            pv = None
            for kk in range(4):
                dc = dq * 4 + kk
                pv = em.op("tensor", lambda e, jb=jb, qb=qb, dc=dc, kk=kk: e.matmul(
                    ps_tr[qb][:, kk * 128:(kk + 1) * 128],
                    xnbf[jb][:, dc * 128:(dc + 1) * 128], ident[:],
                    start=True, stop=True), s_pe)
            tok_xn[jb] = (s_pe, pv)
            em.wait("vector", s_pe, pv)
            dv = em.op("vector", lambda e, j=j, qb=qb, dq=dq: e.tensor_copy(
                xnT[:].rearrange("p (dc t) -> p dc t", dc=NDC)
                [:, dq * 4:(dq + 1) * 4, j * 128:(j + 1) * 128],
                ps_tr[qb][:].rearrange("p (k t) -> p k t", k=4)), s_dve)
            tok_pstr[qb] = (s_dve, dv)
        if j == 0:
            for dq in range(4):
                qb = dq % 2
                em.wait("tensor", s_dve, av)
                em.wait("tensor", *tok_pstr[qb])
                pv = None
                for kk in range(4):
                    dc = dq * 4 + kk
                    pv = em.op("tensor", lambda e, qb=qb, dc=dc, kk=kk: e.matmul(
                        ps_tr[qb][:, kk * 128:(kk + 1) * 128],
                        xnlon[:, dc * 128:(dc + 1) * 128], ident[:],
                        start=True, stop=True), s_pe)
                em.wait("vector", s_pe, pv)
                dv = em.op("vector", lambda e, qb=qb, dq=dq: e.tensor_copy(
                    xnloT[:].rearrange("p (dc t) -> p dc t", dc=NDC)
                    [:, dq * 4:(dq + 1) * 4, :],
                    ps_tr[qb][:].rearrange("p (k t) -> p k t", k=4)), s_dve)
                tok_pstr[qb] = (s_dve, dv)

    sn_ln = snap()
    for e_ in ("tensor", "vector", "scalar", "sync"):
        em.waitsnap(e_, sn_ln)
    s_dbg = mksem("s_dbg", True)
    if dbg:
        em.op("sync", lambda e: e.dma_start(out=dbg_d["xnT"], in_=xnT[:]), s_dbg, 16)
    if stage == "ln":
        em.wait("sync", s_dbg, 0)
        blk = nc.Block().__enter__()
        em.emit(blk)
        blk.__exit__(None, None, None)
        lnc.close(); ctx.close()
        return nc
    lnc.close()

    gc = ExitStack()
    gsb = lambda name, shape, dt: gc.enter_context(nc.sbuf_tensor(name, shape, dt))
    gps = lambda name, shape: gc.enter_context(nc.psum_tensor(name, shape, F32))
    ybq = gsb("ybq", [128, GH * T], BF16)
    ybk = gsb("ybk", [128, GH * T], BF16)
    vaug = gsb("vaug", [128, NTT * GH * 129], BF16)
    attnS = [gsb(f"attnS_{i}", [128, 512], BF16) for i in range(3)]
    oTg = gsb("oTg_0", [128, T], BF16)
    varq_row = gsb("varq_row", [1, GH * T], BF16)
    vark_row = gsb("vark_row", [1, GH * T], BF16)
    invk_cols = gsb("invk_cols", [128, 64], F32)
    dmask = gsb("dmask_s", [128, 128], BF16)
    em.op("sync", lambda e: e.dma_start(out=dmask[:], in_=dmask_d[:, 0:128]), s_const, 16)
    invk_tmp = gsb("invk_tmp", [128, 16], F32)

    ych = [gsb(f"ych_{i}", [128, 512], BF16) for i in range(2)]
    y2ch = ych
    zt = gsb("zt", [128, 1], F32)
    rec = gsb("rec", [128, 1], F32)
    onat = [gsb(f"onat_{i}", [128, 128], BF16) for i in range(4)]
    yq0f = gsb("yq0f", [128, GH * 128], F32)
    ych0f = gsb("ych0f", [128, 128], F32)
    wlo = [gsb(f"wlo_{i}", [128, 2048], BF16) for i in range(3)]
    wslab32 = gsb("wslab32", [128, 1024], F32)
    corrf = gsb("corrf", [128, 128], F32)
    mu0f = gsb("mu0f", [128, 128], F32)
    yk0f = gsb("yk0f", [128, GH * 128], F32)
    attn0b = gsb("attn0b", [128, 128], BF16)
    rbc = [0]
    pending_post = []
    pbk = [gps(f"pb_{i}", [128, 512]) for i in range(8)]
    ps_y = [pbk[0], pbk[1]]
    ps_rb = [pbk[2], pbk[3], pbk[4]]
    ps_oz = {0: pbk[5], 1: pbk[6], 2: pbk[7], 3: pbk[0]}
    s_wsl = [mksem(f"s_wsl{i}", True) for i in range(3)]
    s_w32 = mksem("s_w32", True)
    s_vsl = mksem("s_vsl", True)
    s_osp2 = [mksem(f"s_osp{i}", True) for i in range(2)]
    s_dbg2 = mksem("s_dbg2", True)

    for g in range(NG):
        jobs = [("k", wk_d, ybk, vark_row, hl) for hl in range(GH)] + \
               [("q", wq_d, ybq, varq_row, hl) for hl in range(GH)]

        def emit_wload(idx, g=g, jobs=jobs):
            side, wsrc, ybuf, vrow, hl = jobs[idx]
            wb = idx % 3
            ec = g * GH + hl
            em.wait("gpsimd", *gtok(f"wslab{wb}"))
            em.op("gpsimd", lambda e, wb=wb, wsrc=wsrc, ec=ec: e.dma_start(
                out=wslab[wb][:], in_=wsrc[ec]), s_wsl[wb], 16)

        def emit_wload_g(gx, idx):
            wb = idx % 3
            wsrc = wk_d if idx < GH else wq_d
            ec = gx * GH + (idx % GH)
            em.wait("gpsimd", *gtok(f"wslab{wb}"))
            em.op("gpsimd", lambda e, wb=wb, wsrc=wsrc, ec=ec: e.dma_start(
                out=wslab[wb][:], in_=wsrc[ec]), s_wsl[wb], 16)

        def emit_wlo_half(gx, idx, hf):
            side = "k" if idx < GH else "q"
            hl_ = idx % GH
            wsrc = wk_d if side == "k" else wq_d
            w3 = idx % 3
            wb = idx % 3
            ec = gx * GH + hl_
            em.wait("scalar", *gtok("wslab32"))
            em.op("scalar", lambda e, wsrc=wsrc, ec=ec, hf=hf: e.dma_start(
                out=wslab32[:], in_=wsrc[ec][:, hf * 1024:(hf + 1) * 1024]),
                s_w32, 16)
            em.wait("vector", s_w32, 0)
            em.wait("vector", s_wsl[wb], 0)
            if hf == 0:
                em.wait("vector", *gtok(f"wlo{w3}"))
            dv = em.op("vector", lambda e, w3=w3, wb=wb, hf=hf: e.tensor_sub(
                wlo[w3][:, hf * 1024:(hf + 1) * 1024], wslab32[:],
                wslab[wb][:, hf * 1024:(hf + 1) * 1024]), s_dve)
            tok["wslab32"] = (s_dve, dv)
            if hf == 1:
                tok[f"wlo_rdy{w3}"] = (s_dve, dv)

        if g == 0:
            emit_wload(0)
            emit_wload(1)
        for _job_i in range(len(jobs)):
            if _job_i + 2 < len(jobs):
                emit_wload(_job_i + 2)
            side, wsrc, ybuf, vrow, hl = jobs[_job_i]
            if True:
                ec = g * GH + hl
                wb = _job_i % 3
                pv_last = None
                ymm_pv = {}

                def emit_ymm(tc, wb=wb):
                    cb = tc % 2
                    em.wait("tensor", s_wsl[wb], 0)
                    em.wait("tensor", *gtok(f"pb{cb}"))
                    pv = None
                    for dc in range(NDC):
                        pv = em.op("tensor", lambda e, cb=cb, wb=wb, dc=dc, tc=tc: e.matmul(
                            ps_y[cb][:], wslab[wb][:, dc * 128:(dc + 1) * 128],
                            xnT[:, dc * T + tc * 512: dc * T + tc * 512 + 512],
                            start=(dc == 0), stop=(dc == NDC - 1)),
                            s_pe if dc == NDC - 1 else None)
                    ymm_pv[tc] = pv
                    return pv

                def emit_stats(tc, side=side, ybuf=ybuf, vrow=vrow, hl=hl):
                    cb = tc % 2
                    pv = ymm_pv[tc]
                    em.wait("scalar", s_pe, pv)
                    em.wait("scalar", *gtok(f"ych{cb}"))
                    av = em.op("scalar", lambda e, cb=cb: e.activation(
                        ych[cb][:], ps_y[cb][:], AF.Copy, scale=gam[:]), s_act)
                    if tc == 0:
                        em.wait("scalar", *gtok("ych0f"))
                        av = em.op("scalar", lambda e, cb=cb: e.activation(
                            ych0f[:], ps_y[cb][:, 0:128], AF.Copy, scale=gam[:]), s_act)
                    tok[f"pb{cb}"] = (s_act, av)
                    em.wait("tensor", s_act, av)
                    em.wait("tensor", *gtok(f"pb{cb + 2}"))
                    pv = em.op("tensor", lambda e, cb=cb: e.matmul(
                        ps_rb[cb][:], onedm[:], ych[cb][:], start=True, stop=True), s_pe)
                    em.wait("vector", s_pe, pv)
                    dv = em.op("vector", lambda e, cb=cb, hl=hl, tc=tc, ybuf=ybuf: e.scalar_tensor_tensor(
                        ybuf[:, hl * T + tc * 512: hl * T + tc * 512 + 512],
                        ych[cb][:], betc[:], ps_rb[cb][:], ALU.add, ALU.subtract), s_dve)
                    if tc == 0:
                        dv = em.op("vector", lambda e, cb=cb: e.tensor_copy(
                            mu0f[:], ps_rb[cb][:, 0:128]), s_dve)
                    tok[f"ych{cb}"] = (s_dve, dv)
                    tok[f"pb{cb + 2}"] = (s_dve, dv)
                    em.wait("scalar", s_dve, dv)
                    em.wait("scalar", *gtok(f"y2ch{cb}"))
                    av = em.op("scalar", lambda e, cb=cb, hl=hl, tc=tc, ybuf=ybuf: e.activation(
                        y2ch[cb][:], ybuf[:, hl * T + tc * 512: hl * T + tc * 512 + 512],
                        AF.Square), s_act)
                    em.wait("tensor", s_act, av)
                    em.wait("tensor", *gtok(f"pb{cb + 2}"))
                    pv = em.op("tensor", lambda e, cb=cb: e.matmul(
                        ps_rb[cb][:], onedm[:], y2ch[cb][:], start=True, stop=True), s_pe)
                    tok[f"y2ch{cb}"] = (s_pe, pv)
                    tok[f"ych{cb}"] = (s_pe, pv)
                    em.wait("scalar", s_pe, pv)
                    vsc = (EPS * HD) / (2.0 ** -10) if side == "q" else 1.0
                    av = em.op("scalar", lambda e, cb=cb, hl=hl, tc=tc, vrow=vrow, vsc=vsc: e.activation(
                        vrow[0:1, hl * T + tc * 512: hl * T + tc * 512 + 512],
                        ps_rb[cb][0:1, :], AF.Copy, scale=vsc), s_act)
                    tok[f"pb{cb + 2}"] = (s_act, av)

                if g == 0 and _job_i == 0:
                    for hf in (0, 1):
                        emit_wlo_half(0, 0, hf)
                        emit_wlo_half(0, 1, hf)
                for tc in range(NTC):
                    pv_last = emit_ymm(tc)
                    if tc >= 1:
                        emit_stats(tc - 1)
                        if tc in (1, 3) and _job_i + 2 < len(jobs):
                            emit_wlo_half(g, _job_i + 2, tc // 2)
                emit_stats(NTC - 1)

                em.wait("tensor", *gtok(f"wlo_rdy{_job_i % 3}"))
                em.wait("tensor", *gtok("pb6"))
                pv = None
                for dc in range(NDC):
                    em.op("tensor", lambda e, wb=wb, dc=dc: e.matmul(
                        ps_oz[1][:, 0:128], wslab[wb][:, dc * 128:(dc + 1) * 128],
                        xnloT[:, dc * 128:(dc + 1) * 128],
                        start=(dc == 0), stop=False))
                for dc in range(NDC):
                    pv = em.op("tensor", lambda e, w3=_job_i % 3, dc=dc: e.matmul(
                        ps_oz[1][:, 0:128], wlo[w3][:, dc * 128:(dc + 1) * 128],
                        xnT[:, dc * T: dc * T + 128],
                        start=False, stop=(dc == NDC - 1)),
                        s_pe if dc == NDC - 1 else None)
                em.wait("scalar", s_pe, pv)
                av = em.op("scalar", lambda e: e.activation(
                    corrf[:], ps_oz[1][:, 0:128], AF.Copy, scale=gam[:]), s_act)
                tok["pb6"] = (s_act, av)
                em.wait("vector", s_act, av)
                dv = em.op("vector", lambda e: e.tensor_add(
                    corrf[:], ych0f[:], corrf[:]), s_dve)
                em.wait("vector", s_dve, dv)
                y0f = yk0f if side == "k" else yq0f
                dv = em.op("vector", lambda e, hl=hl, y0f=y0f: e.scalar_tensor_tensor(
                    y0f[:, hl * 128:(hl + 1) * 128], corrf[:],
                    betc[:], mu0f[:], ALU.add, ALU.subtract), s_dve)
                tok["ych0f"] = (s_dve, dv)
                tok[f"wslab{wb}"] = (s_pe, pv)
                tok[f"wlo{_job_i % 3}"] = (s_pe, pv)
        vark_done = s_act.n
        for hl in range(GH):
            em.wait("tensor", s_act, vark_done)
            em.wait("tensor", *gtok("pb5"))
            pv = None
            for tt in range(NTT):
                pv = em.op("tensor", lambda e, hl=hl, tt=tt: e.matmul(
                    ps_oz[0][:, tt:tt + 1],
                    vark_row[0:1, hl * T + tt * 128: hl * T + tt * 128 + 128],
                    one11[:], start=True, stop=True), s_pe)
            em.wait("vector", s_pe, pv)
            dv = em.op("vector", lambda e: e.tensor_scalar_add(
                invk_tmp[:], ps_oz[0][:, 0:16], EPS), s_dve)
            tok["pb5"] = (s_dve, dv)
            em.wait("vector", s_dve, dv)
            dv = em.op("vector", lambda e, hl=hl: e.reciprocal(
                invk_cols[:, hl * 16: hl * 16 + 16], invk_tmp[:]), s_dve)
        invk_done = s_dve.n
        wv_view = wv_d[g].rearrange("p (dc m) -> p dc m", dc=NDC)
        for eh in range(2):
            em.wait("gpsimd", *gtok("vslab"))
            em.op("gpsimd", lambda e, wvv=wv_view, eh=eh: e.dma_start(
                out=vslab[:].rearrange("p (dc m) -> p dc m", dc=NDC),
                in_=wvv[:, :, eh * 256:(eh + 1) * 256]), s_vsl, 16)
            pv = None
            for tt in range(NTT):
                vb = tt % 2
                em.wait("tensor", s_vsl, 0)
                em.wait("tensor", *gtok(f"pb{vb}"))
                for dc in range(NDC):
                    pv = em.op("tensor", lambda e, vb=vb, dc=dc, tt=tt: e.matmul(
                        ps_y[vb][:, 0:256],
                        xnT[:, dc * T + tt * 128: dc * T + tt * 128 + 128],
                        vslab[:, dc * 256:(dc + 1) * 256],
                        start=(dc == 0), stop=(dc == NDC - 1)),
                        s_pe if dc == NDC - 1 else None)
                if tt == NTT - 1:
                    tok["vslab"] = (s_pe, pv)
                em.wait("vector", s_pe, pv)
                em.wait("vector", s_dve, invk_done)
                dv = None
                for hh in range(2):
                    hl = eh * 2 + hh
                    off = tt * (GH * 129) + hl * 129
                    dv = em.op("vector", lambda e, vb=vb, off=off, hh=hh, hl=hl, tt=tt: e.tensor_scalar_mul(
                        vaug[:, off: off + 128], ps_y[vb][:, hh * 128:(hh + 1) * 128],
                        invk_cols[:, hl * 16 + tt: hl * 16 + tt + 1]), s_dve)
                    dv = em.op("vector", lambda e, off=off, hl=hl, tt=tt: e.tensor_copy(
                        vaug[:, off + 128: off + 129],
                        invk_cols[:, hl * 16 + tt: hl * 16 + tt + 1]), s_dve)
                tok[f"pb{vb}"] = (s_dve, dv)

        qkv_done = snap()
        if dbg and g == 0:
            for e_ in ENGINES:
                em.waitsnap(e_, qkv_done)
            for nm, src in (("ybq", ybq), ("ybk", ybk), ("vaug", vaug),
                            ("invk", invk_cols), ("varq", varq_row)):
                em.op("sync", lambda e, nm=nm, src=src: e.dma_start(
                    out=dbg_d[nm], in_=src[:]), s_dbg2, 16)
        if stage == "qkv" and g == 0:
            break

        for e_ in ENGINES:
            em.waitsnap(e_, qkv_done)
        ozbank = {0: 5, 1: 6, 2: 7, 3: 0}
        RBL = (2, 3, 4, 1)
        oz_stop_pv = {}
        post_state = {}
        pv_grp = None

        for hl in range(GH):
            for c in range(NTC):
                nj = 4 * c + 4

                def emit_dots(j, hl=hl, c=c):
                    bk = RBL[rbc[0] % 4]
                    rbc[0] += 1
                    r = j % 3
                    em.wait("tensor", *gtok(f"pb{bk}"))
                    pv = em.op("tensor", lambda e, bk=bk, hl=hl, j=j, c=c: e.matmul(
                        pbk[bk][:], ybk[:, hl * T + j * 128: hl * T + j * 128 + 128],
                        ybq[:, hl * T + c * 512: hl * T + c * 512 + 512],
                        start=True, stop=True), s_pe)
                    jl = j - 4 * c
                    if c == 0 and j == 0:
                        em.wait("scalar", s_pe, pv)
                        em.wait("scalar", *gtok("aSfree0"))
                        av = em.op("scalar", lambda e, bk=bk: e.activation(
                            attnS[0][:, 128:512], pbk[bk][:, 128:512],
                            AF.Square), s_act)
                        tok[f"pb{bk}"] = (s_act, av)
                        tok["aS0"] = (s_act, av)
                        em.wait("tensor", *gtok(f"pb{bk}"))
                        pv = em.op("tensor", lambda e, bk=bk, hl=hl: e.matmul(
                            pbk[bk][:, 0:128], yk0f[:, hl * 128:(hl + 1) * 128],
                            yq0f[:, hl * 128:(hl + 1) * 128],
                            start=True, stop=True), s_pe)
                        em.wait("scalar", s_pe, pv)
                        em.wait("scalar", *gtok("attn0b"))
                        av = em.op("scalar", lambda e, bk=bk: e.activation(
                            attn0b[:], pbk[bk][:, 0:128], AF.Square), s_act)
                        em.wait("vector", s_act, av)
                        dvm = em.op("vector", lambda e: e.tensor_mul(
                            attn0b[:], attn0b[:], dmask[:]), s_dve)
                        tok[f"pb{bk}"] = (s_act, av)
                        tok["attn0f_rdy"] = (s_dve, dvm)
                    elif jl >= 0:
                        em.wait("scalar", s_pe, pv)
                        em.wait("scalar", *gtok(f"aSfree{r}"))
                        av = em.op("scalar", lambda e, bk=bk, r=r: e.activation(
                            attnS[r][:], pbk[bk][:], AF.Square), s_act)
                        em.wait("vector", s_act, av)
                        dvm = em.op("vector", lambda e, r=r, jl=jl: e.tensor_mul(
                            attnS[r][:, jl * 128: jl * 128 + 128],
                            attnS[r][:, jl * 128: jl * 128 + 128],
                            dmask[:]), s_dve)
                        tok[f"pb{bk}"] = (s_act, av)
                        tok[f"aS{r}"] = (s_dve, dvm)
                    else:
                        em.wait("scalar", s_pe, pv)
                        em.wait("scalar", *gtok(f"aSfree{r}"))
                        av = em.op("scalar", lambda e, bk=bk, r=r: e.activation(
                            attnS[r][:], pbk[bk][:], AF.Square), s_act)
                        tok[f"pb{bk}"] = (s_act, av)
                        tok[f"aS{r}"] = (s_act, av)

                def emit_ozb(j, hl=hl, c=c):
                    r = j % 3
                    lo = max(0, j - 4 * c)
                    em.wait("tensor", *gtok(f"aS{j % 3}" if not (c == 0 and j == 0)
                                            else "attn0f_rdy"))
                    if c == 0 and j == 0:
                        em.wait("tensor", *gtok("aS0"))
                    if j == 0:
                        for il in range(4):
                            em.wait("tensor", *gtok(f"pb{ozbank[il]}"))
                    pv = None
                    for il in range(lo, 4):
                        if c == 0 and j == 0 and il == 0:
                            pv = em.op("tensor", lambda e, hl=hl: e.matmul(
                                ps_oz[0][:, 0:129], attn0b[:],
                                vaug[:, hl * 129:(hl + 1) * 129],
                                start=True, stop=False, skip_group_check=True),
                                s_pe)
                            continue
                        pv = em.op("tensor", lambda e, il=il, r=r, j=j, hl=hl: e.matmul(
                            ps_oz[il][:, 0:129],
                            attnS[r][:, il * 128: il * 128 + 128],
                            vaug[:, j * (GH * 129) + hl * 129:
                                 j * (GH * 129) + hl * 129 + 129],
                            start=(j == 0), stop=False, skip_group_check=True),
                            s_pe)
                    if c == 0 and j == 0:
                        tok["attn0b"] = (s_pe, pv)
                    tok[f"aSfree{r}"] = (s_pe, pv)
                    return pv

                def emit_stop(il, hl=hl, c=c):
                    i = 4 * c + il
                    pv = em.op("tensor", lambda e, il=il, hl=hl, i=i: e.matmul(
                        ps_oz[il][:, 0:129],
                        varq_row[0:1, hl * T + i * 128: hl * T + i * 128 + 128],
                        ecol[:], start=False, stop=True, skip_group_check=True),
                        s_pe)
                    oz_stop_pv[il] = pv
                    return pv

                def emit_post_dve(il, hl=hl, c=c):
                    pv = oz_stop_pv[il]
                    pr = il
                    em.wait("vector", s_pe, pv)
                    em.wait("vector", *gtok("zt"))
                    dv = em.op("vector", lambda e, il=il: e.tensor_scalar_add(
                        zt[:], ps_oz[il][:, 128:129], EPS * EPS * HD), s_dve)
                    em.wait("vector", s_dve, dv)
                    dv = em.op("vector", lambda e: e.reciprocal(rec[:], zt[:]), s_dve)
                    tok["zt"] = (s_dve, dv)
                    em.wait("vector", s_dve, dv)
                    em.wait("vector", *gtok(f"onat{pr}"))
                    dv = em.op("vector", lambda e, il=il, pr=pr: e.tensor_scalar_mul(
                        onat[pr][:], ps_oz[il][:, 0:128], rec[:]), s_dve)
                    tok[f"pb{ozbank[il]}"] = (s_dve, dv)
                    post_state[(hl, c, il)] = dv

                def emit_post_tr(il, hl=hl, c=c, g=g):
                    i = 4 * c + il
                    pr = il
                    bk = RBL[rbc[0] % 4]
                    rbc[0] += 1
                    em.wait("tensor", s_dve, post_state[(hl, c, il)])
                    em.wait("tensor", *gtok(f"pb{bk}"))
                    pv = em.op("tensor", lambda e, pr=pr, bk=bk: e.matmul(
                        pbk[bk][:, 0:128], onat[pr][:], ident[:],
                        start=True, stop=True), s_pe)
                    tok[f"onat{pr}"] = (s_pe, pv)
                    em.wait("scalar", s_pe, pv)
                    if c == 0 and il == 0:
                        em.wait("scalar", *gtok("oTgS"))
                        em.wait("scalar", *gtok("oTgS2"))
                    av = em.op("scalar", lambda e, i=i, bk=bk: e.activation(
                        oTg[:, i * 128: i * 128 + 128],
                        pbk[bk][:, 0:128], AF.Copy), s_act)
                    tok[f"pb{bk}"] = (s_act, av)

                for j in range(nj):
                    if j == 2 and pending_post:
                        for fn in pending_post:
                            fn()
                        pending_post[:] = []
                    emit_dots(j)
                    if j >= 2:
                        emit_ozb(j - 2)
                emit_ozb(nj - 2)
                emit_ozb(nj - 1)
                for il in range(4):
                    pv_grp = emit_stop(il)
                for il in range(4):
                    emit_post_dve(il)
                pending_post[:] = [
                    (lambda il=il, f=emit_post_tr: f(il)) for il in range(4)]
                tok["grp_free"] = (s_pe, pv_grp)
            for fn in pending_post:
                fn()
            pending_post[:] = []
            em.wait("sync", s_act, s_act.n)
            em.op("sync", lambda e, g=g, hl=hl: e.dma_start(
                out=oT_scr[(g * GH + hl) * 128:(g * GH + hl) * 128 + 128, :],
                in_=oTg[:]), s_osp2[0], 16)
            tok["oTgS"] = (s_osp2[0], 0)
            if dbg and g == 0 and hl == 0:
                em.op("sync", lambda e: e.dma_start(
                    out=dbg_d["oTg"], in_=oTg[:]), s_dbg2, 16)
                tok["oTgS2"] = (s_dbg2, 0)
        if g < NG - 1:
            emit_wload_g(g + 1, 0)
            emit_wload_g(g + 1, 1)
            for hf in (0, 1):
                emit_wlo_half(g + 1, 0, hf)
                emit_wlo_half(g + 1, 1, hf)
            for e_ in ("tensor", "vector", "scalar"):
                em.wait(e_, *gtok("grp_free"))
        else:
            barrier()
        if stage == "attn" and g == 0:
            break

    if stage in ("qkv", "attn"):
        em.wait("sync", s_dbg2, 0)
        em.wait("sync", s_osp2[0], 0)
        em.wait("sync", s_osp2[1], 0)
        blk = nc.Block().__enter__()
        em.emit(blk)
        blk.__exit__(None, None, None)
        gc.close()
        ctx.close()
        return nc
    gc.close()

    woc = ExitStack()
    wsb = lambda name, shape, dt: woc.enter_context(nc.sbuf_tensor(name, shape, dt))
    wps = lambda name, shape: woc.enter_context(nc.psum_tensor(name, shape, F32))
    wo_sb = wsb("wo_sb", [128, 16 * 2048], BF16)
    oTsl = [wsb(f"oTsl_{i}", [128, 16 * 512], BF16) for i in range(2)]
    accst = [wsb(f"accst_{i}", [128, 2048], F32) for i in range(2)]
    ps_ao = [wps(f"ps_ao_{i}", [128, 512]) for i in range(4)]
    s_wo = mksem("s_wo", True)
    s_otl = [mksem(f"s_otl{i}", True) for i in range(2)]
    s_xr = [mksem(f"s_xr{i}", True) for i in range(2)]
    s_acst = [mksem(f"s_acst{i}", True) for i in range(2)]

    for ec in range(16):
        em.op("gpsimd", lambda e, ec=ec: e.dma_start(
            out=wo_sb[:, ec * 2048:(ec + 1) * 2048],
            in_=wo_d[ec * 128:(ec + 1) * 128, :]), s_wo, 16)
    oT_view = oT_scr.rearrange("(ec p) t -> p ec t", p=128)
    for tc in range(NTC):
        ob = tc % 2
        em.wait("sync", *gtok(f"oTsl{ob}"))
        for hf in range(2):
            em.op("sync", lambda e, ob=ob, tc=tc, hf=hf: e.dma_start(
                out=oTsl[ob][:].rearrange("p (ec t) -> p ec t", ec=16)
                [:, hf * 8:(hf + 1) * 8, :],
                in_=oT_view[:, hf * 8:(hf + 1) * 8, tc * 512:(tc + 1) * 512]),
                s_otl[ob], 16)
        pv_last = None
        for ttl in range(4):
            tt = tc * 4 + ttl
            xb = tt % 2
            em.wait("sync", *gtok(f"accst{xb}"))
            em.wait("sync", *gtok(f"xres{xb}"))
            em.op("sync", lambda e, xb=xb, tt=tt: e.dma_start(
                out=accst[xb][:], in_=x_d[tt * 128:(tt + 1) * 128, :]), s_xr[xb], 16)
            for ddc in range(4):
                pb = ddc % 4
                em.wait("tensor", s_wo, 0)
                em.wait("tensor", s_otl[ob], 0)
                em.wait("tensor", *gtok(f"ps_ao{pb}"))
                pv = None
                for ec in range(16):
                    pv = em.op("tensor", lambda e, pb=pb, ob=ob, ec=ec, ttl=ttl, ddc=ddc: e.matmul(
                        ps_ao[pb][:], oTsl[ob][:, ec * 512 + ttl * 128: ec * 512 + ttl * 128 + 128],
                        wo_sb[:, ec * 2048 + ddc * 512: ec * 2048 + ddc * 512 + 512],
                        start=(ec == 0), stop=(ec == 15)), s_pe if ec == 15 else None)
                pv_last = pv
                em.wait("vector", s_pe, pv)
                em.wait("vector", s_xr[xb], 0)
                dv = em.op("vector", lambda e, pb=pb, xb=xb, ddc=ddc: e.tensor_add(
                    accst[xb][:, ddc * 512:(ddc + 1) * 512], ps_ao[pb][:],
                    accst[xb][:, ddc * 512:(ddc + 1) * 512]), s_dve)
                tok[f"ps_ao{pb}"] = (s_dve, dv)
                if ddc == 3:
                    tok[f"xres{xb}"] = (s_dve, dv)
            em.wait("sync", s_dve, s_dve.n)
            em.op("sync", lambda e, xb=xb, tt=tt: e.dma_start(
                out=acc_scr[tt * 128:(tt + 1) * 128, :], in_=accst[xb][:]), s_acst[xb], 16)
            tok[f"accst{xb}"] = (s_acst[xb], 0)
        tok[f"oTsl{ob}"] = (s_pe, pv_last)
    barrier()
    woc.close()
    if stage == "wo":
        blk = nc.Block().__enter__()
        em.emit(blk)
        blk.__exit__(None, None, None)
        ctx.close()
        return nc

    fc = ExitStack()
    fsb = lambda name, shape, dt: fc.enter_context(nc.sbuf_tensor(name, shape, dt))
    fps = lambda name, shape: fc.enter_context(nc.psum_tensor(name, shape, F32))
    w1sl = [fsb(f"w1sl_{i}", [128, 2048], BF16) for i in range(2)]
    w3sl = [fsb(f"w3sl_{i}", [128, 2048], BF16) for i in range(2)]
    silust = [fsb(f"silust_{i}", [128, 512], F32) for i in range(2)]
    hidst = [fsb(f"hidst_{i}", [128, 512], BF16) for i in range(2)]
    ust = [fsb(f"ust_{i}", [128, 512], BF16) for i in range(2)]
    ps_g = [fps(f"ps_g_{i}", [128, 512]) for i in range(2)]
    ps_u3 = [fps(f"ps_u3_{i}", [128, 512]) for i in range(2)]
    ps_uh = [fps(f"ps_uh_{i}", [128, 512]) for i in range(2)]
    s_wA = [mksem(f"s_wA{i}", True) for i in range(2)]
    s_ust = [mksem(f"s_ust{i}", True) for i in range(2)]

    def emit_wAload(blk_i):
        bb = blk_i % 2
        em.wait("gpsimd", *gtok(f"w1sl{bb}"))
        em.op("gpsimd", lambda e, bb=bb, blk_i=blk_i: e.dma_start(
            out=w1sl[bb][:], in_=w1_d[blk_i]), s_wA[bb], 16)
        em.op("gpsimd", lambda e, bb=bb, blk_i=blk_i: e.dma_start(
            out=w3sl[bb][:], in_=w3_d[blk_i]), s_wA[bb], 16)

    emit_wAload(0)
    up_pv = {}

    def emit_ffn_up(st):
        blk_i, tc = divmod(st, NTC)
        bb = blk_i % 2
        cb = tc % 2
        em.wait("tensor", s_wA[bb], 0)
        em.wait("tensor", *gtok(f"ps_g{cb}"))
        for dc in range(NDC):
            em.op("tensor", lambda e, cb=cb, bb=bb, dc=dc, tc=tc: e.matmul(
                ps_g[cb][:], w1sl[bb][:, dc * 128:(dc + 1) * 128],
                xnT[:, dc * T + tc * 512: dc * T + tc * 512 + 512],
                start=(dc == 0), stop=(dc == NDC - 1)),
                s_pe if dc == NDC - 1 else None)
        em.wait("tensor", *gtok(f"ps_u3{cb}"))
        pv3 = None
        for dc in range(NDC):
            pv3 = em.op("tensor", lambda e, cb=cb, bb=bb, dc=dc, tc=tc: e.matmul(
                ps_u3[cb][:], w3sl[bb][:, dc * 128:(dc + 1) * 128],
                xnT[:, dc * T + tc * 512: dc * T + tc * 512 + 512],
                start=(dc == 0), stop=(dc == NDC - 1)),
                s_pe if dc == NDC - 1 else None)
        up_pv[st] = pv3
        if tc == NTC - 1:
            tok[f"w1sl{bb}"] = (s_pe, pv3)

    def emit_ffn_epi(st):
        blk_i, tc = divmod(st, NTC)
        cb = tc % 2
        pv3 = up_pv[st]
        em.wait("scalar", s_pe, pv3)
        em.wait("scalar", *gtok(f"silust{cb}"))
        av = em.op("scalar", lambda e, cb=cb: e.activation(
            silust[cb][:], ps_g[cb][:], AF.Silu), s_act)
        tok[f"ps_g{cb}"] = (s_act, av)
        em.wait("vector", s_act, av)
        em.wait("vector", *gtok(f"hidst{cb}"))
        dv = em.op("vector", lambda e, cb=cb: e.tensor_mul(
            hidst[cb][:], silust[cb][:], ps_u3[cb][:]), s_dve)
        tok[f"ps_u3{cb}"] = (s_dve, dv)
        tok[f"silust{cb}"] = (s_dve, dv)
        em.wait("tensor", s_dve, dv)
        em.wait("tensor", *gtok(f"ps_uh{cb}"))
        pv = em.op("tensor", lambda e, cb=cb: e.matmul(
            ps_uh[cb][:], h128[:], hidst[cb][:], start=True, stop=True), s_pe)
        tok[f"hidst{cb}"] = (s_pe, pv)
        em.wait("scalar", s_pe, pv)
        em.wait("scalar", *gtok(f"ust{cb}"))
        av = em.op("scalar", lambda e, cb=cb: e.activation(
            ust[cb][:], ps_uh[cb][:], AF.Copy), s_act)
        tok[f"ps_uh{cb}"] = (s_act, av)
        em.wait("sync", s_act, av)
        em.op("sync", lambda e, cb=cb, blk_i=blk_i, tc=tc: e.dma_start(
            out=u_scr[blk_i * 128:(blk_i + 1) * 128, tc * 512:(tc + 1) * 512],
            in_=ust[cb][:]), s_ust[cb], 16)
        tok[f"ust{cb}"] = (s_ust[cb], 0)

    for st in range(NHB * NTC):
        blk_i, tc = divmod(st, NTC)
        if tc == 0 and blk_i + 1 < NHB:
            emit_wAload(blk_i + 1)
        emit_ffn_up(st)
        if st >= 1:
            emit_ffn_epi(st - 1)
    emit_ffn_epi(NHB * NTC - 1)
    barrier()
    fc.close()
    if stage == "ffna":
        blk = nc.Block().__enter__()
        em.emit(blk)
        blk.__exit__(None, None, None)
        ctx.close()
        return nc

    bc = ExitStack()
    bsb = lambda name, shape, dt: bc.enter_context(nc.sbuf_tensor(name, shape, dt))
    bps = lambda name, shape: bc.enter_context(nc.psum_tensor(name, shape, F32))
    u_sl = bsb("u_sl", [128, NHB * 512], BF16)
    acc_sl = bsb("acc_sl", [128, 4 * 1024], F32)
    w2sl = [bsb(f"w2sl_{i}", [128, 2048], BF16) for i in range(3)]
    outst = [bsb(f"outst_{i}", [128, 1024], F32) for i in range(4)]
    ps_o = [bps(f"ps_o_{i}", [128, 512]) for i in range(8)]
    s_usl = mksem("s_usl", True)
    s_accl = mksem("s_accl", True)
    s_w2s = [mksem(f"s_w2s{i}", True) for i in range(3)]
    s_uslB = mksem("s_uslB", True)
    s_outB = mksem("s_outB", True)
    u_view = u_scr.rearrange("(hb p) t -> p hb t", p=128)
    acc_view = acc_scr.rearrange("(tcg ttl p) d -> tcg p ttl d", ttl=4, p=128)

    def emit_uload(tc, half):
        em.wait("sync", *gtok("u_slA" if half == 0 else "u_slB"))
        for part in range(4 * half, 4 * half + 4):
            em.op("sync", lambda e, tc=tc, part=part: e.dma_start(
                out=u_sl[:].rearrange("p (hb t) -> p hb t", hb=NHB)
                [:, part * 8:(part + 1) * 8, :],
                in_=u_view[:, part * 8:(part + 1) * 8, tc * 512:(tc + 1) * 512]),
                s_usl if half == 0 else s_uslB, 16)

    emit_uload(0, 0)
    emit_uload(0, 1)
    for tc in range(NTC):
        dv_last = None
        NQ = NHB // 2

        def emit_w2sload(qd, ddh):
            wb = (ddh * NQ + qd) % 3
            em.wait("scalar", *gtok(f"w2sl{wb}"))
            em.op("scalar", lambda e, wb=wb, qd=qd, ddh=ddh: e.dma_start(
                out=w2sl[wb][:].rearrange("p (b d) -> p b d", b=2),
                in_=w2pp_scr.rearrange("(b p) d -> p b d", p=128)
                [:, qd * 2:(qd + 1) * 2, ddh * 1024:(ddh + 1) * 1024]),
                s_w2s[wb], 16)

        for ddh in range(2):
            em.wait("sync", *gtok("acc_sl"))
            em.op("sync", lambda e, tc=tc, ddh=ddh: e.dma_start(
                out=acc_sl[:].rearrange("p (ttl d) -> p ttl d", ttl=4),
                in_=acc_view[tc][:, :, ddh * 1024:(ddh + 1) * 1024]), s_accl, 16)
            pv_fin = None
            if ddh == 0:
                emit_w2sload(0, 0)
                emit_w2sload(1, 0)
            for qd in range(NQ):
                wb = (ddh * NQ + qd) % 3
                if qd + 2 < NQ:
                    emit_w2sload(qd + 2, ddh)
                elif ddh == 0:
                    emit_w2sload(qd + 2 - NQ, 1)
                em.wait("tensor", s_w2s[wb], 0)
                if qd == 0:
                    em.wait("tensor", s_usl, 0)
                    for k in range(8):
                        em.wait("tensor", *gtok(f"ps_o{k}"))
                elif qd == NQ // 2:
                    em.wait("tensor", s_uslB, 0)
                pv = None
                for hq in range(2):
                    hb = qd * 2 + hq
                    for ttl in range(4):
                        for dd5 in range(2):
                            pv = em.op("tensor", lambda e, ttl=ttl, dd5=dd5, wb=wb, hb=hb, hq=hq: e.matmul(
                                ps_o[ttl * 2 + dd5][:],
                                u_sl[:, hb * 512 + ttl * 128: hb * 512 + ttl * 128 + 128],
                                w2sl[wb][:, hq * 1024 + dd5 * 512: hq * 1024 + dd5 * 512 + 512],
                                start=(hb == 0), stop=(hb == NHB - 1),
                                skip_group_check=True), s_pe)
                tok[f"w2sl{wb}"] = (s_pe, pv)
                pv_fin = pv
                if ddh == 1 and qd == NQ // 2 - 1:
                    tok["u_slA"] = (s_pe, pv)
                    if tc + 1 < NTC:
                        emit_uload(tc + 1, 0)
            if ddh == 1:
                tok["u_slB"] = (s_pe, pv_fin)
                if tc + 1 < NTC:
                    emit_uload(tc + 1, 1)
            for ttl in range(4):
                ttl_dv = None
                for dd5 in range(2):
                    k = ttl * 2 + dd5
                    em.wait("vector", s_pe, pv_fin)
                    em.wait("vector", s_accl, 0)
                    if dd5 == 0:
                        em.wait("vector", *gtok(f"outst{ttl}"))
                    dv = em.op("vector", lambda e, k=k, ttl=ttl, dd5=dd5: e.tensor_add(
                        outst[ttl][:, dd5 * 512: dd5 * 512 + 512],
                        ps_o[k][:],
                        acc_sl[:, ttl * 1024 + dd5 * 512:
                               ttl * 1024 + dd5 * 512 + 512]), s_dve)
                    tok[f"ps_o{k}"] = (s_dve, dv)
                    ttl_dv = dv
                    dv_last = dv
                tt = tc * 4 + ttl
                em.wait("sync", s_dve, ttl_dv)
                em.op("sync", lambda e, ttl=ttl, tt=tt, ddh=ddh: e.dma_start(
                    out=out_d[tt * 128:(tt + 1) * 128, ddh * 1024:(ddh + 1) * 1024],
                    in_=outst[ttl][:]), s_outB, 16)
                tok[f"outst{ttl}"] = (s_outB, 0)
            tok["acc_sl"] = (s_dve, dv_last)

    em.wait("sync", s_outB, 0)
    if dbg:
        em.wait("sync", s_dbg, 0)
        em.wait("sync", s_dbg2, 0)
    blk = nc.Block().__enter__()
    em.emit(blk)
    blk.__exit__(None, None, None)
    bc.close()
    ctx.close()
    return nc


def hmat(n):
    h = np.array([[1.0]], dtype=np.float64)
    while h.shape[0] < n:
        h = np.block([[h, h], [h, -h]])
    return h


def prep_inputs(inputs):
    x = np.ascontiguousarray(np.asarray(inputs["L0_attn_hidden_states"]))
    Wq = np.asarray(inputs["L0_attn_Wq"]); Wk = np.asarray(inputs["L0_attn_Wk"])
    Wv = np.asarray(inputs["L0_attn_Wv"]); Wo = np.asarray(inputs["L0_attn_Wo"])
    gamma = np.asarray(inputs["L0_attn_gamma"]); beta = np.asarray(inputs["L0_attn_beta"])
    W1 = np.asarray(inputs["L0_ffn_W1"]); W2 = np.asarray(inputs["L0_ffn_W2"])
    W3 = np.asarray(inputs["L0_ffn_W3"])

    bf = ml_dtypes.bfloat16
    blk16 = lambda W: np.ascontiguousarray(
        W.reshape(16, 128, 16, 128).transpose(0, 3, 2, 1)).reshape(16, 128, 2048)
    wq = blk16(Wq)
    wk = blk16(Wk)
    wv = np.ascontiguousarray(
        Wv.reshape(4, 512, 16, 128).transpose(0, 3, 2, 1)).reshape(4, 128, 8192)
    wo = np.ascontiguousarray(Wo.T)
    blk64 = lambda W: np.ascontiguousarray(
        W.reshape(64, 128, 16, 128).transpose(0, 3, 2, 1)).reshape(64, 128, 2048)
    w1 = blk64(W1)
    w3 = blk64(W3)
    w2T = np.ascontiguousarray(W2.T)
    perm = np.arange(NH)
    perm = (perm % 64) * 128 + (perm // 64)
    w2p = np.ascontiguousarray(w2T[perm])

    consts = {
        "ident": np.eye(128).astype(bf),
        "h128": hmat(128).astype(bf),
        "h64d": np.kron(np.eye(2), hmat(64)).astype(bf),
        "oneD": np.full((128, 1), 1.0 / 128.0).astype(bf),
        "onedm": np.full((128, 128), 1.0 / 128.0).astype(bf),
        "dmask": np.concatenate(
            [np.concatenate([np.triu(np.ones((128, 128))) if il == jl else
                             np.ones((128, 128)) for il in range(4)], axis=1)
             for jl in range(4)], axis=1).astype(bf),
        "onesr": np.ones((1, 128)).astype(bf),
        "one11": np.ones((1, 1)).astype(bf),
        "ecol": np.concatenate([np.zeros(128), [2.0 ** -10]]).reshape(1, 129).astype(bf),
        "gamma": np.ascontiguousarray(np.asarray(gamma, np.float32).reshape(128, 1)),
        "beta": np.ascontiguousarray(np.asarray(beta, np.float32).reshape(128, 1)),
    }
    shared = dict(wq=wq, wk=wk, wv=wv, wo=wo, w1=w1, w3=w3, w2p=w2p, **consts)
    in_maps = []
    for b in range(x.shape[0]):
        m = dict(shared)
        m["x"] = np.ascontiguousarray(x[b])
        in_maps.append(m)
    return in_maps


from concourse.bass_utils import run_bass_kernel_spmd


def kernel(**inputs):
    nc = build_program(stage="full", dbg=False)
    in_maps = prep_inputs(inputs)
    res = run_bass_kernel_spmd(nc, in_maps, core_ids=list(range(8)))
    return np.stack([np.asarray(r["out"], dtype=np.float32) for r in res.results],
                    axis=0)


# version 21
# speedup vs baseline: 1.1029x; 1.1029x over previous
import numpy as np
import ml_dtypes
from contextlib import ExitStack

import concourse.bass as bass
import concourse.mybir as mybir

F32 = mybir.dt.float32
BF16 = mybir.dt.bfloat16
AF = mybir.ActivationFunctionType
ALU = mybir.AluOpType
AX = mybir.AxisListType

EPS = 1e-5
T = 2048
D = 2048
NH = 8192
HD = 128
NHEADS = 16
NG = 4
GH = 4
NTT = 16
NTC = 4
NDC = 16
NHB = 64
RSQ = float(1.0 / np.sqrt(NH))

ENGINES = ("sync", "gpsimd", "tensor", "vector", "scalar")


class Sem:
    __slots__ = ("h", "n", "name", "is_dma")

    def __init__(self, ctx, nc, name, is_dma=False):
        self.h = ctx.enter_context(nc.semaphore(name))
        self.n = 0
        self.name = name
        self.is_dma = is_dma


class Em:

    def __init__(self):
        self.q = {e: [] for e in ENGINES}
        self.own = {}

    def wait(self, eng, sem, val):
        if sem.is_dma:
            val = sem.n
        if val > 0:
            q = self.q[eng]
            if q and q[-1][0] == "w" and q[-1][1] is sem:
                if q[-1][2] >= val:
                    return
                q.pop()
            q.append(("w", sem, val))

    def waitsnap(self, eng, snapshot):
        for sem, val in snapshot:
            self.wait(eng, sem, val)

    def op(self, eng, fn, inc=None, k=1):
        own = self.own.get(eng)
        if own is not None:
            self.wait(eng, own, own.n)
            if inc is None:
                inc = own
        if inc is None:
            self.q[eng].append(("o", fn, None, 0))
            return None
        inc.n += k
        self.q[eng].append(("o", fn, inc, k))
        return inc.n

    def emit(self, block):
        for ename in ENGINES:
            items = self.q[ename]

            def run(eng, items=items):
                for it in items:
                    if it[0] == "w":
                        eng.wait_ge(it[1].h, it[2])
                    else:
                        instr = it[1](eng)
                        if it[2] is not None:
                            instr.then_inc(it[2].h, it[3])
            getattr(block, ename)(run)


def build_program(stage="full", dbg=False):
    nc = bass.Bass("TRN2", target_bir_lowering=False, debug=False)
    scr_kind = "ExternalOutput" if dbg else "Internal"

    dr = lambda name, shape, dt, kind: nc.dram_tensor(name, shape, dt, kind=kind).ap()
    x_d = dr("x", [T, D], F32, "ExternalInput")
    wq_d = dr("wq", [16, 128, 2048], F32, "ExternalInput")
    wk_d = dr("wk", [16, 128, 2048], F32, "ExternalInput")
    wv_d = dr("wv", [4, 128, 8192], F32, "ExternalInput")
    wo_d = dr("wo", [2048, 2048], F32, "ExternalInput")
    w1_d = dr("w1", [64, 128, 2048], F32, "ExternalInput")
    w3_d = dr("w3", [64, 128, 2048], F32, "ExternalInput")
    w2p_d = dr("w2p", [8192, 2048], F32, "ExternalInput")
    gamma_d = dr("gamma", [128, 1], F32, "ExternalInput")
    beta_d = dr("beta", [128, 1], F32, "ExternalInput")
    ident_d = dr("ident", [128, 128], BF16, "ExternalInput")
    h128_d = dr("h128", [128, 128], BF16, "ExternalInput")
    h64d_d = dr("h64d", [128, 128], BF16, "ExternalInput")
    oneD_d = dr("oneD", [128, 1], BF16, "ExternalInput")
    onedm_d = dr("onedm", [128, 128], BF16, "ExternalInput")
    dmask_d = dr("dmask", [128, 4 * 512], BF16, "ExternalInput")
    onesr_d = dr("onesr", [1, 128], BF16, "ExternalInput")
    one11_d = dr("one11", [1, 1], BF16, "ExternalInput")
    ecol_d = dr("ecol", [1, 129], BF16, "ExternalInput")

    out_d = dr("out", [T, D], F32, "ExternalOutput")
    u_scr = dr("u_scr", [NH, T], BF16, scr_kind)
    w2pp_scr = dr("w2pp_scr", [NH, D], BF16, scr_kind)
    oT_scr = dr("oT_scr", [D, T], BF16, scr_kind)
    acc_scr = dr("acc_scr", [T, D], F32, scr_kind)
    dbg_d = {}
    if dbg:
        dbg_d["xnT"] = dr("dbg_xnT", [128, NDC * T], BF16, "ExternalOutput")
        dbg_d["ybq"] = dr("dbg_ybq", [128, GH * T], BF16, "ExternalOutput")
        dbg_d["ybk"] = dr("dbg_ybk", [128, GH * T], BF16, "ExternalOutput")
        dbg_d["vaug"] = dr("dbg_vaug", [128, NTT * GH * 129], BF16, "ExternalOutput")
        dbg_d["invk"] = dr("dbg_invk", [128, 64], F32, "ExternalOutput")
        dbg_d["varq"] = dr("dbg_varq", [1, GH * T], BF16, "ExternalOutput")
        dbg_d["oTg"] = dr("dbg_oTg", [128, T], BF16, "ExternalOutput")

    ctx = ExitStack()
    sb = lambda name, shape, dt: ctx.enter_context(nc.sbuf_tensor(name, shape, dt))

    xnT = sb("xnT", [128, NDC * T], BF16)
    xnloT = sb("xnloT", [128, NDC * 128], BF16)
    wslab = [sb(f"wslab_{i}", [128, 2048], BF16) for i in range(3)]
    vslab = sb("vslab", [128, 4096], BF16)
    ident = sb("ident_s", [128, 128], BF16)
    h128 = sb("h128_s", [128, 128], BF16)
    h64d = sb("h64d_s", [128, 128], BF16)
    oneD = sb("oneD_s", [128, 1], BF16)
    onedm = sb("onedm_s", [128, 128], BF16)
    onesr = sb("onesr_s", [1, 128], BF16)
    one11 = sb("one11_s", [1, 1], BF16)
    ecol = sb("ecol_s", [1, 129], BF16)
    gam = sb("gam_s", [128, 1], F32)
    bet = sb("bet_s", [128, 1], F32)
    betc = sb("betc_s", [128, 1], F32)
    betbf = sb("betbf_s", [128, 1], BF16)
    meanb = sb("meanb_s", [1, 1], BF16)

    em = Em()
    ALL = []

    def mksem(name, is_dma=False):
        s_ = Sem(ctx, nc, name, is_dma)
        ALL.append(s_)
        return s_

    s_pe = mksem("s_pe")
    s_dve = mksem("s_dve")
    s_act = mksem("s_act")
    em.own["vector"] = s_dve
    em.own["scalar"] = s_act

    def snap():
        return [(s, s.n) for s in ALL]

    def barrier():
        sn = snap()
        for e in ENGINES:
            em.waitsnap(e, sn)

    s_const = mksem("s_const", True)
    for dst, src in ((ident, ident_d), (h128, h128_d), (h64d, h64d_d),
                     (oneD, oneD_d), (onedm, onedm_d), (onesr, onesr_d),
                     (one11, one11_d), (ecol, ecol_d), (gam, gamma_d), (bet, beta_d)):
        em.op("sync", lambda e, d=dst, s=src: e.dma_start(out=d[:], in_=s), s_const, 16)
    barrier()

    tok = {}

    def gtok(name):
        return tok.get(name, (s_pe, 0))

    lnc = ExitStack()
    lsb = lambda name, shape, dt: lnc.enter_context(nc.sbuf_tensor(name, shape, dt))
    lps = lambda name, shape: lnc.enter_context(nc.psum_tensor(name, shape, F32))
    xt32 = [lsb(f"xt32_{i}", [128, 2048], F32) for i in range(2)]
    xsqt = lsb("xsqt", [128, 2048], F32)
    xn32t = lsb("xn32t", [128, 2048], F32)
    xnlon = lsb("xnlon", [128, 2048], BF16)
    xnbf = [lsb(f"xnbf_{i}", [128, 2048], BF16) for i in range(2)]
    st_c = {n: [lsb(f"ln_{n}_{i}", [128, 1], F32) for i in range(2)]
            for n in ("xsum", "xss", "mu", "var", "rstd", "nb")}
    ps_tr = [lps(f"ps_tr_{i}", [128, 512]) for i in range(2)]
    s_xt = [mksem(f"s_xt{i}", True) for i in range(2)]

    dv = em.op("vector", lambda e: e.tensor_copy(betbf[:], bet[:]), s_dve)
    em.wait("tensor", s_dve, dv)
    pv = em.op("tensor", lambda e: e.matmul(
        ps_tr[0][0:1, 0:1], oneD[:], betbf[:], start=True, stop=True), s_pe)
    em.wait("scalar", s_pe, pv)
    av = em.op("scalar", lambda e: e.activation(
        meanb[:], ps_tr[0][0:1, 0:1], AF.Copy), s_act)
    em.wait("tensor", s_act, av)
    pv = em.op("tensor", lambda e: e.matmul(
        ps_tr[0][:, 0:1], onesr[:], meanb[:], start=True, stop=True), s_pe)
    em.wait("vector", s_pe, pv)
    dv = em.op("vector", lambda e: e.tensor_sub(betc[:], bet[:], ps_tr[0][:, 0:1]), s_dve)

    tok_xt = [(s_dve, 0), (s_dve, 0)]
    tok_xn = [(s_pe, 0), (s_pe, 0)]
    tok_pstr = [(s_dve, dv), (s_dve, 0)]

    w2pb = [lsb(f"w2pb_{i}", [128, 2048], BF16) for i in range(4)]
    w2pb32 = [lsb(f"w2pb32_{i}", [128, 2048], F32) for i in range(2)]
    foldst = [lsb(f"foldst_{i}", [128, 2048], BF16) for i in range(2)]
    ps_f = [lps(f"ps_f_{i}", [128, 512]) for i in range(2)]
    s_w2p = [mksem(f"s_w2p{i}", True) for i in range(4)]
    s_w2p32 = [mksem(f"s_w2p32_{i}", True) for i in range(2)]
    s_fst = [mksem(f"s_fst{i}", True) for i in range(2)]
    w2pp_view = w2pp_scr.rearrange("(a bb) d -> bb a d", bb=128)

    def emit_w2pload(blk_i):
        bb = blk_i % 4
        if blk_i % 2 == 0:
            em.wait("gpsimd", *gtok(f"w2pb{bb}"))
            em.op("gpsimd", lambda e, bb=bb, blk_i=blk_i: e.dma_start(
                out=w2pb[bb][:], in_=w2p_d[blk_i * 128:(blk_i + 1) * 128, :]),
                s_w2p[bb], 16)
        else:
            st = (blk_i % 4) // 2
            em.wait("sync", *gtok(f"w2pb32_{st}"))
            em.op("sync", lambda e, st=st, blk_i=blk_i: e.dma_start(
                out=w2pb32[st][:], in_=w2p_d[blk_i * 128:(blk_i + 1) * 128, :]),
                s_w2p32[st], 16)
            em.wait("vector", s_w2p32[st], 0)
            em.wait("vector", *gtok(f"w2pb{bb}"))
            dv = em.op("vector", lambda e, bb=bb, st=st: e.tensor_copy(
                w2pb[bb][:], w2pb32[st][:]), s_dve)
            tok[f"w2pb32_{st}"] = (s_dve, dv)
            tok[f"w2pb{bb}_rdy"] = (s_dve, dv)

    def emit_fold_job(blk_i):
        bb = blk_i % 4
        fs = blk_i % 2
        if blk_i + 3 < NHB:
            emit_w2pload(blk_i + 3)
        dv_last = None
        pv_last = None
        for ddc in range(4):
            pb = ddc % 2
            if blk_i % 2 == 0:
                em.wait("tensor", s_w2p[bb], 0)
            else:
                em.wait("tensor", *gtok(f"w2pb{bb}_rdy"))
            em.wait("tensor", *gtok(f"ps_f{pb}"))
            pv = em.op("tensor", lambda e, pb=pb, bb=bb, ddc=ddc: e.matmul(
                ps_f[pb][:], h64d[:], w2pb[bb][:, ddc * 512:(ddc + 1) * 512],
                start=True, stop=True), s_pe)
            pv_last = pv
            em.wait("vector", s_pe, pv)
            if ddc == 0:
                em.wait("vector", *gtok(f"foldst{fs}"))
            dv = em.op("vector", lambda e, pb=pb, fs=fs, ddc=ddc: e.tensor_scalar_mul(
                foldst[fs][:, ddc * 512:(ddc + 1) * 512], ps_f[pb][:], RSQ), s_dve)
            tok[f"ps_f{pb}"] = (s_dve, dv)
            dv_last = dv
        tok[f"w2pb{bb}"] = (s_pe, pv_last)
        q = "scalar"
        em.wait(q, s_dve, dv_last)
        for bit in range(2):
            em.op(q, lambda e, blk_i=blk_i, bit=bit, fs=fs: e.dma_start(
                out=w2pp_view[2 * blk_i + bit],
                in_=foldst[fs][bit * 64:(bit + 1) * 64, :]), s_fst[fs], 16)
        tok[f"foldst{fs}"] = (s_fst[fs], 0)

    emit_w2pload(0)
    emit_w2pload(1)
    emit_w2pload(2)

    def emit_xload(j):
        jb = j % 2
        em.wait("scalar", *tok_xt[jb])
        em.op("scalar", lambda e, j=j, jb=jb: e.dma_start(
            out=xt32[jb][:], in_=x_d[j * 128:(j + 1) * 128, :]), s_xt[jb], 16)

    emit_xload(0)
    for j in range(NTT):
        if j + 1 < NTT:
            emit_xload(j + 1)
        for _fj in range(4):
            emit_fold_job(j * 4 + _fj)
        jb = j % 2
        em.wait("vector", s_xt[jb], 0)
        rv = em.op("vector", lambda e, jb=jb: e.reduce_sum(
            st_c["xsum"][jb][:], xt32[jb][:], axis=AX.X), s_dve)
        em.wait("scalar", s_xt[jb], 0)
        av = em.op("scalar", lambda e, jb=jb: e.activation(
            xsqt[:], xt32[jb][:], AF.Square, accum_out=st_c["xss"][jb][:]), s_act)
        em.wait("vector", s_act, av)
        em.wait("vector", s_dve, rv)
        em.op("vector", lambda e, jb=jb: e.tensor_scalar_mul(
            st_c["mu"][jb][:], st_c["xsum"][jb][:], 1.0 / 2048.0))
        em.op("vector", lambda e, jb=jb: e.tensor_scalar_mul(
            st_c["var"][jb][:], st_c["xss"][jb][:], 1.0 / 2048.0))
        dv = em.op("vector", lambda e, jb=jb: e.scalar_tensor_tensor(
            st_c["rstd"][jb][:], st_c["mu"][jb][:], -1.0, st_c["mu"][jb][:],
            ALU.mult, ALU.mult), s_dve)
        em.wait("vector", s_dve, dv)
        dv = em.op("vector", lambda e, jb=jb: e.tensor_add(
            st_c["var"][jb][:], st_c["var"][jb][:], st_c["rstd"][jb][:]), s_dve)
        em.wait("vector", s_dve, dv)
        dv = em.op("vector", lambda e, jb=jb: e.tensor_scalar_add(
            st_c["var"][jb][:], st_c["var"][jb][:], EPS), s_dve)
        em.wait("scalar", s_dve, dv)
        av = em.op("scalar", lambda e, jb=jb: e.activation(
            st_c["rstd"][jb][:], st_c["var"][jb][:], AF.Ln), s_act)
        em.wait("scalar", s_act, av)
        av = em.op("scalar", lambda e, jb=jb: e.activation(
            st_c["rstd"][jb][:], st_c["rstd"][jb][:], AF.Exp, scale=-0.5), s_act)
        em.wait("vector", s_act, av)
        dv = em.op("vector", lambda e, jb=jb: e.scalar_tensor_tensor(
            st_c["nb"][jb][:], st_c["mu"][jb][:], -1.0, st_c["rstd"][jb][:],
            ALU.mult, ALU.mult), s_dve)
        em.wait("vector", s_dve, dv)
        em.wait("vector", *tok_xn[jb])
        av = em.op("vector", lambda e, jb=jb: e.tensor_scalar(
            xnbf[jb][:], xt32[jb][:], st_c["rstd"][jb][:], st_c["nb"][jb][:],
            ALU.mult, ALU.add), s_dve)
        if j == 0:
            av = em.op("vector", lambda e, jb=jb: e.tensor_scalar(
                xn32t[:], xt32[jb][:], st_c["rstd"][jb][:], st_c["nb"][jb][:],
                ALU.mult, ALU.add), s_dve)
            em.wait("vector", s_dve, av)
            av = em.op("vector", lambda e, jb=jb: e.tensor_sub(
                xnlon[:], xn32t[:], xnbf[jb][:]), s_dve)
        tok_xt[jb] = (s_dve, av)
        for dq in range(4):
            qb = dq % 2
            em.wait("tensor", s_dve, av)
            em.wait("tensor", *tok_pstr[qb])
            pv = None
            for kk in range(4):
                dc = dq * 4 + kk
                pv = em.op("tensor", lambda e, jb=jb, qb=qb, dc=dc, kk=kk: e.matmul(
                    ps_tr[qb][:, kk * 128:(kk + 1) * 128],
                    xnbf[jb][:, dc * 128:(dc + 1) * 128], ident[:],
                    start=True, stop=True), s_pe)
            tok_xn[jb] = (s_pe, pv)
            em.wait("vector", s_pe, pv)
            dv = em.op("vector", lambda e, j=j, qb=qb, dq=dq: e.tensor_copy(
                xnT[:].rearrange("p (dc t) -> p dc t", dc=NDC)
                [:, dq * 4:(dq + 1) * 4, j * 128:(j + 1) * 128],
                ps_tr[qb][:].rearrange("p (k t) -> p k t", k=4)), s_dve)
            tok_pstr[qb] = (s_dve, dv)
        if j == 0:
            for dq in range(4):
                qb = dq % 2
                em.wait("tensor", s_dve, av)
                em.wait("tensor", *tok_pstr[qb])
                pv = None
                for kk in range(4):
                    dc = dq * 4 + kk
                    pv = em.op("tensor", lambda e, qb=qb, dc=dc, kk=kk: e.matmul(
                        ps_tr[qb][:, kk * 128:(kk + 1) * 128],
                        xnlon[:, dc * 128:(dc + 1) * 128], ident[:],
                        start=True, stop=True), s_pe)
                em.wait("vector", s_pe, pv)
                dv = em.op("vector", lambda e, qb=qb, dq=dq: e.tensor_copy(
                    xnloT[:].rearrange("p (dc t) -> p dc t", dc=NDC)
                    [:, dq * 4:(dq + 1) * 4, :],
                    ps_tr[qb][:].rearrange("p (k t) -> p k t", k=4)), s_dve)
                tok_pstr[qb] = (s_dve, dv)

    sn_ln = snap()
    for e_ in ("tensor", "vector", "scalar", "sync"):
        em.waitsnap(e_, sn_ln)
    s_dbg = mksem("s_dbg", True)
    if dbg:
        em.op("sync", lambda e: e.dma_start(out=dbg_d["xnT"], in_=xnT[:]), s_dbg, 16)
    if stage == "ln":
        em.wait("sync", s_dbg, 0)
        blk = nc.Block().__enter__()
        em.emit(blk)
        blk.__exit__(None, None, None)
        lnc.close(); ctx.close()
        return nc
    lnc.close()

    gc = ExitStack()
    gsb = lambda name, shape, dt: gc.enter_context(nc.sbuf_tensor(name, shape, dt))
    gps = lambda name, shape: gc.enter_context(nc.psum_tensor(name, shape, F32))
    ybq = gsb("ybq", [128, GH * T], BF16)
    ybk = gsb("ybk", [128, GH * T], BF16)
    vaug = gsb("vaug", [128, NTT * GH * 129], BF16)
    attnS = [gsb(f"attnS_{i}", [128, 512], BF16) for i in range(3)]
    oTg = gsb("oTg_0", [128, T], BF16)
    varq_row = gsb("varq_row", [1, GH * T], BF16)
    vark_row = gsb("vark_row", [1, GH * T], BF16)
    invk_cols = gsb("invk_cols", [128, 64], F32)
    dmask = gsb("dmask_s", [128, 128], BF16)
    em.op("sync", lambda e: e.dma_start(out=dmask[:], in_=dmask_d[:, 0:128]), s_const, 16)
    invk_tmp = gsb("invk_tmp", [128, 16], F32)

    ych = [gsb(f"ych_{i}", [128, 512], BF16) for i in range(2)]
    y2ch = ych
    zt = gsb("zt", [128, 1], F32)
    rec = gsb("rec", [128, 1], F32)
    onat = [gsb(f"onat_{i}", [128, 128], BF16) for i in range(4)]
    yq0f = gsb("yq0f", [128, GH * 128], F32)
    ych0f = gsb("ych0f", [128, 128], F32)
    wlo = [gsb(f"wlo_{i}", [128, 2048], BF16) for i in range(3)]
    wslab32 = gsb("wslab32", [128, 1024], F32)
    corrf = gsb("corrf", [128, 128], F32)
    mu0f = gsb("mu0f", [128, 128], F32)
    yk0f = gsb("yk0f", [128, GH * 128], F32)
    attn0b = gsb("attn0b", [128, 128], BF16)
    rbc = [0]
    pending_post = []
    pbk = [gps(f"pb_{i}", [128, 512]) for i in range(8)]
    ps_y = [pbk[0], pbk[1]]
    ps_rb = [pbk[2], pbk[3], pbk[4]]
    ps_oz = {0: pbk[5], 1: pbk[6], 2: pbk[7], 3: pbk[0]}
    s_wsl = [mksem(f"s_wsl{i}", True) for i in range(3)]
    s_w32 = mksem("s_w32", True)
    s_vsl = mksem("s_vsl", True)
    s_osp2 = [mksem(f"s_osp{i}", True) for i in range(2)]
    s_dbg2 = mksem("s_dbg2", True)

    for g in range(NG):
        jobs = [("k", wk_d, ybk, vark_row, hl) for hl in range(GH)] + \
               [("q", wq_d, ybq, varq_row, hl) for hl in range(GH)]

        def emit_wload(idx, g=g, jobs=jobs):
            side, wsrc, ybuf, vrow, hl = jobs[idx]
            wb = idx % 3
            ec = g * GH + hl
            em.wait("gpsimd", *gtok(f"wslab{wb}"))
            em.op("gpsimd", lambda e, wb=wb, wsrc=wsrc, ec=ec: e.dma_start(
                out=wslab[wb][:], in_=wsrc[ec]), s_wsl[wb], 16)

        def emit_wload_g(gx, idx):
            wb = idx % 3
            wsrc = wk_d if idx < GH else wq_d
            ec = gx * GH + (idx % GH)
            em.wait("gpsimd", *gtok(f"wslab{wb}"))
            em.op("gpsimd", lambda e, wb=wb, wsrc=wsrc, ec=ec: e.dma_start(
                out=wslab[wb][:], in_=wsrc[ec]), s_wsl[wb], 16)

        def emit_wlo_half(gx, idx, hf):
            side = "k" if idx < GH else "q"
            hl_ = idx % GH
            wsrc = wk_d if side == "k" else wq_d
            w3 = idx % 3
            wb = idx % 3
            ec = gx * GH + hl_
            em.wait("scalar", *gtok("wslab32"))
            em.op("scalar", lambda e, wsrc=wsrc, ec=ec, hf=hf: e.dma_start(
                out=wslab32[:], in_=wsrc[ec][:, hf * 1024:(hf + 1) * 1024]),
                s_w32, 16)
            em.wait("vector", s_w32, 0)
            em.wait("vector", s_wsl[wb], 0)
            if hf == 0:
                em.wait("vector", *gtok(f"wlo{w3}"))
            dv = em.op("vector", lambda e, w3=w3, wb=wb, hf=hf: e.tensor_sub(
                wlo[w3][:, hf * 1024:(hf + 1) * 1024], wslab32[:],
                wslab[wb][:, hf * 1024:(hf + 1) * 1024]), s_dve)
            tok["wslab32"] = (s_dve, dv)
            if hf == 1:
                tok[f"wlo_rdy{w3}"] = (s_dve, dv)

        if g == 0:
            emit_wload(0)
            emit_wload(1)
        for _job_i in range(len(jobs)):
            if _job_i + 2 < len(jobs):
                emit_wload(_job_i + 2)
            side, wsrc, ybuf, vrow, hl = jobs[_job_i]
            if True:
                ec = g * GH + hl
                wb = _job_i % 3
                pv_last = None
                ymm_pv = {}

                def emit_ymm(tc, wb=wb):
                    cb = tc % 2
                    em.wait("tensor", s_wsl[wb], 0)
                    em.wait("tensor", *gtok(f"pb{cb}"))
                    pv = None
                    for dc in range(NDC):
                        pv = em.op("tensor", lambda e, cb=cb, wb=wb, dc=dc, tc=tc: e.matmul(
                            ps_y[cb][:], wslab[wb][:, dc * 128:(dc + 1) * 128],
                            xnT[:, dc * T + tc * 512: dc * T + tc * 512 + 512],
                            start=(dc == 0), stop=(dc == NDC - 1)),
                            s_pe if dc == NDC - 1 else None)
                    ymm_pv[tc] = pv
                    return pv

                sq_av = {}

                def emit_statsA(tc, side=side, ybuf=ybuf, vrow=vrow, hl=hl):
                    cb = tc % 2
                    pv = ymm_pv[tc]
                    em.wait("scalar", s_pe, pv)
                    em.wait("scalar", *gtok(f"ych{cb}"))
                    av = em.op("scalar", lambda e, cb=cb: e.activation(
                        ych[cb][:], ps_y[cb][:], AF.Copy, scale=gam[:]), s_act)
                    if tc == 0:
                        em.wait("scalar", *gtok("ych0f"))
                        av = em.op("scalar", lambda e, cb=cb: e.activation(
                            ych0f[:], ps_y[cb][:, 0:128], AF.Copy, scale=gam[:]), s_act)
                    tok[f"pb{cb}"] = (s_act, av)
                    em.wait("tensor", s_act, av)
                    em.wait("tensor", *gtok(f"pb{cb + 2}"))
                    pv = em.op("tensor", lambda e, cb=cb: e.matmul(
                        ps_rb[cb][:], onedm[:], ych[cb][:], start=True, stop=True), s_pe)
                    em.wait("vector", s_pe, pv)
                    dv = em.op("vector", lambda e, cb=cb, hl=hl, tc=tc, ybuf=ybuf: e.scalar_tensor_tensor(
                        ybuf[:, hl * T + tc * 512: hl * T + tc * 512 + 512],
                        ych[cb][:], betc[:], ps_rb[cb][:], ALU.add, ALU.subtract), s_dve)
                    if tc == 0:
                        dv = em.op("vector", lambda e, cb=cb: e.tensor_copy(
                            mu0f[:], ps_rb[cb][:, 0:128]), s_dve)
                    tok[f"ych{cb}"] = (s_dve, dv)
                    tok[f"pb{cb + 2}"] = (s_dve, dv)
                    em.wait("scalar", s_dve, dv)
                    em.wait("scalar", *gtok(f"y2ch{cb}"))
                    av = em.op("scalar", lambda e, cb=cb, hl=hl, tc=tc, ybuf=ybuf: e.activation(
                        y2ch[cb][:], ybuf[:, hl * T + tc * 512: hl * T + tc * 512 + 512],
                        AF.Square), s_act)
                    sq_av[tc] = av

                def emit_statsB(tc, side=side, ybuf=ybuf, vrow=vrow, hl=hl):
                    cb = tc % 2
                    em.wait("tensor", s_act, sq_av[tc])
                    em.wait("tensor", *gtok(f"pb{cb + 2}"))
                    pv = em.op("tensor", lambda e, cb=cb: e.matmul(
                        ps_rb[cb][:], onedm[:], y2ch[cb][:], start=True, stop=True), s_pe)
                    tok[f"y2ch{cb}"] = (s_pe, pv)
                    tok[f"ych{cb}"] = (s_pe, pv)
                    em.wait("scalar", s_pe, pv)
                    vsc = (EPS * HD) / (2.0 ** -10) if side == "q" else 1.0
                    av = em.op("scalar", lambda e, cb=cb, hl=hl, tc=tc, vrow=vrow, vsc=vsc: e.activation(
                        vrow[0:1, hl * T + tc * 512: hl * T + tc * 512 + 512],
                        ps_rb[cb][0:1, :], AF.Copy, scale=vsc), s_act)
                    tok[f"pb{cb + 2}"] = (s_act, av)

                if g == 0 and _job_i == 0:
                    for hf in (0, 1):
                        emit_wlo_half(0, 0, hf)
                        emit_wlo_half(0, 1, hf)
                for tc in range(NTC):
                    pv_last = emit_ymm(tc)
                    if tc >= 1:
                        emit_statsA(tc - 1)
                        if tc in (1, 3) and _job_i + 2 < len(jobs):
                            emit_wlo_half(g, _job_i + 2, tc // 2)
                    if tc >= 2:
                        emit_statsB(tc - 2)
                emit_statsA(NTC - 1)
                emit_statsB(NTC - 2)
                emit_statsB(NTC - 1)

                em.wait("tensor", *gtok(f"wlo_rdy{_job_i % 3}"))
                em.wait("tensor", *gtok("pb6"))
                pv = None
                for dc in range(NDC):
                    em.op("tensor", lambda e, wb=wb, dc=dc: e.matmul(
                        ps_oz[1][:, 0:128], wslab[wb][:, dc * 128:(dc + 1) * 128],
                        xnloT[:, dc * 128:(dc + 1) * 128],
                        start=(dc == 0), stop=False))
                for dc in range(NDC):
                    pv = em.op("tensor", lambda e, w3=_job_i % 3, dc=dc: e.matmul(
                        ps_oz[1][:, 0:128], wlo[w3][:, dc * 128:(dc + 1) * 128],
                        xnT[:, dc * T: dc * T + 128],
                        start=False, stop=(dc == NDC - 1)),
                        s_pe if dc == NDC - 1 else None)
                em.wait("scalar", s_pe, pv)
                av = em.op("scalar", lambda e: e.activation(
                    corrf[:], ps_oz[1][:, 0:128], AF.Copy, scale=gam[:]), s_act)
                tok["pb6"] = (s_act, av)
                em.wait("vector", s_act, av)
                dv = em.op("vector", lambda e: e.tensor_add(
                    corrf[:], ych0f[:], corrf[:]), s_dve)
                em.wait("vector", s_dve, dv)
                y0f = yk0f if side == "k" else yq0f
                dv = em.op("vector", lambda e, hl=hl, y0f=y0f: e.scalar_tensor_tensor(
                    y0f[:, hl * 128:(hl + 1) * 128], corrf[:],
                    betc[:], mu0f[:], ALU.add, ALU.subtract), s_dve)
                tok["ych0f"] = (s_dve, dv)
                tok[f"wslab{wb}"] = (s_pe, pv)
                tok[f"wlo{_job_i % 3}"] = (s_pe, pv)
        vark_done = s_act.n
        for hl in range(GH):
            em.wait("tensor", s_act, vark_done)
            em.wait("tensor", *gtok("pb5"))
            pv = None
            for tt in range(NTT):
                pv = em.op("tensor", lambda e, hl=hl, tt=tt: e.matmul(
                    ps_oz[0][:, tt:tt + 1],
                    vark_row[0:1, hl * T + tt * 128: hl * T + tt * 128 + 128],
                    one11[:], start=True, stop=True), s_pe)
            em.wait("vector", s_pe, pv)
            dv = em.op("vector", lambda e: e.tensor_scalar_add(
                invk_tmp[:], ps_oz[0][:, 0:16], EPS), s_dve)
            tok["pb5"] = (s_dve, dv)
            em.wait("vector", s_dve, dv)
            dv = em.op("vector", lambda e, hl=hl: e.reciprocal(
                invk_cols[:, hl * 16: hl * 16 + 16], invk_tmp[:]), s_dve)
        invk_done = s_dve.n
        wv_view = wv_d[g].rearrange("p (dc m) -> p dc m", dc=NDC)
        for eh in range(2):
            em.wait("gpsimd", *gtok("vslab"))
            em.op("gpsimd", lambda e, wvv=wv_view, eh=eh: e.dma_start(
                out=vslab[:].rearrange("p (dc m) -> p dc m", dc=NDC),
                in_=wvv[:, :, eh * 256:(eh + 1) * 256]), s_vsl, 16)
            pv = None
            for tt in range(NTT):
                vb = tt % 2
                em.wait("tensor", s_vsl, 0)
                em.wait("tensor", *gtok(f"pb{vb}"))
                for dc in range(NDC):
                    pv = em.op("tensor", lambda e, vb=vb, dc=dc, tt=tt: e.matmul(
                        ps_y[vb][:, 0:256],
                        xnT[:, dc * T + tt * 128: dc * T + tt * 128 + 128],
                        vslab[:, dc * 256:(dc + 1) * 256],
                        start=(dc == 0), stop=(dc == NDC - 1)),
                        s_pe if dc == NDC - 1 else None)
                if tt == NTT - 1:
                    tok["vslab"] = (s_pe, pv)
                em.wait("vector", s_pe, pv)
                em.wait("vector", s_dve, invk_done)
                dv = None
                for hh in range(2):
                    hl = eh * 2 + hh
                    off = tt * (GH * 129) + hl * 129
                    dv = em.op("vector", lambda e, vb=vb, off=off, hh=hh, hl=hl, tt=tt: e.tensor_scalar_mul(
                        vaug[:, off: off + 128], ps_y[vb][:, hh * 128:(hh + 1) * 128],
                        invk_cols[:, hl * 16 + tt: hl * 16 + tt + 1]), s_dve)
                    dv = em.op("vector", lambda e, off=off, hl=hl, tt=tt: e.tensor_copy(
                        vaug[:, off + 128: off + 129],
                        invk_cols[:, hl * 16 + tt: hl * 16 + tt + 1]), s_dve)
                tok[f"pb{vb}"] = (s_dve, dv)

        qkv_done = snap()
        if dbg and g == 0:
            for e_ in ENGINES:
                em.waitsnap(e_, qkv_done)
            for nm, src in (("ybq", ybq), ("ybk", ybk), ("vaug", vaug),
                            ("invk", invk_cols), ("varq", varq_row)):
                em.op("sync", lambda e, nm=nm, src=src: e.dma_start(
                    out=dbg_d[nm], in_=src[:]), s_dbg2, 16)
        if stage == "qkv" and g == 0:
            break

        for e_ in ENGINES:
            em.waitsnap(e_, qkv_done)
        ozbank = {0: 5, 1: 6, 2: 7, 3: 0}
        RBL = (2, 3, 4, 1)
        oz_stop_pv = {}
        post_state = {}
        pv_grp = None

        for hl in range(GH):
            for c in range(NTC):
                nj = 4 * c + 4

                def emit_dots(j, hl=hl, c=c):
                    bk = RBL[rbc[0] % 4]
                    rbc[0] += 1
                    r = j % 3
                    em.wait("tensor", *gtok(f"pb{bk}"))
                    pv = em.op("tensor", lambda e, bk=bk, hl=hl, j=j, c=c: e.matmul(
                        pbk[bk][:], ybk[:, hl * T + j * 128: hl * T + j * 128 + 128],
                        ybq[:, hl * T + c * 512: hl * T + c * 512 + 512],
                        start=True, stop=True), s_pe)
                    jl = j - 4 * c
                    if c == 0 and j == 0:
                        em.wait("scalar", s_pe, pv)
                        em.wait("scalar", *gtok("aSfree0"))
                        av = em.op("scalar", lambda e, bk=bk: e.activation(
                            attnS[0][:, 128:512], pbk[bk][:, 128:512],
                            AF.Square), s_act)
                        tok[f"pb{bk}"] = (s_act, av)
                        tok["aS0"] = (s_act, av)
                        em.wait("tensor", *gtok(f"pb{bk}"))
                        pv = em.op("tensor", lambda e, bk=bk, hl=hl: e.matmul(
                            pbk[bk][:, 0:128], yk0f[:, hl * 128:(hl + 1) * 128],
                            yq0f[:, hl * 128:(hl + 1) * 128],
                            start=True, stop=True), s_pe)
                        em.wait("scalar", s_pe, pv)
                        em.wait("scalar", *gtok("attn0b"))
                        av = em.op("scalar", lambda e, bk=bk: e.activation(
                            attn0b[:], pbk[bk][:, 0:128], AF.Square), s_act)
                        em.wait("vector", s_act, av)
                        dvm = em.op("vector", lambda e: e.tensor_mul(
                            attn0b[:], attn0b[:], dmask[:]), s_dve)
                        tok[f"pb{bk}"] = (s_act, av)
                        tok["attn0f_rdy"] = (s_dve, dvm)
                    elif jl >= 0:
                        em.wait("scalar", s_pe, pv)
                        em.wait("scalar", *gtok(f"aSfree{r}"))
                        av = em.op("scalar", lambda e, bk=bk, r=r: e.activation(
                            attnS[r][:], pbk[bk][:], AF.Square), s_act)
                        em.wait("vector", s_act, av)
                        dvm = em.op("vector", lambda e, r=r, jl=jl: e.tensor_mul(
                            attnS[r][:, jl * 128: jl * 128 + 128],
                            attnS[r][:, jl * 128: jl * 128 + 128],
                            dmask[:]), s_dve)
                        tok[f"pb{bk}"] = (s_act, av)
                        tok[f"aS{r}"] = (s_dve, dvm)
                    else:
                        em.wait("scalar", s_pe, pv)
                        em.wait("scalar", *gtok(f"aSfree{r}"))
                        av = em.op("scalar", lambda e, bk=bk, r=r: e.activation(
                            attnS[r][:], pbk[bk][:], AF.Square), s_act)
                        tok[f"pb{bk}"] = (s_act, av)
                        tok[f"aS{r}"] = (s_act, av)

                def emit_ozb(j, hl=hl, c=c):
                    r = j % 3
                    lo = max(0, j - 4 * c)
                    em.wait("tensor", *gtok(f"aS{j % 3}" if not (c == 0 and j == 0)
                                            else "attn0f_rdy"))
                    if c == 0 and j == 0:
                        em.wait("tensor", *gtok("aS0"))
                    if j == 0:
                        for il in range(4):
                            em.wait("tensor", *gtok(f"pb{ozbank[il]}"))
                    pv = None
                    for il in range(lo, 4):
                        if c == 0 and j == 0 and il == 0:
                            pv = em.op("tensor", lambda e, hl=hl: e.matmul(
                                ps_oz[0][:, 0:129], attn0b[:],
                                vaug[:, hl * 129:(hl + 1) * 129],
                                start=True, stop=False, skip_group_check=True),
                                s_pe)
                            continue
                        pv = em.op("tensor", lambda e, il=il, r=r, j=j, hl=hl: e.matmul(
                            ps_oz[il][:, 0:129],
                            attnS[r][:, il * 128: il * 128 + 128],
                            vaug[:, j * (GH * 129) + hl * 129:
                                 j * (GH * 129) + hl * 129 + 129],
                            start=(j == 0), stop=False, skip_group_check=True),
                            s_pe)
                    if c == 0 and j == 0:
                        tok["attn0b"] = (s_pe, pv)
                    tok[f"aSfree{r}"] = (s_pe, pv)
                    return pv

                def emit_stop(il, hl=hl, c=c):
                    i = 4 * c + il
                    pv = em.op("tensor", lambda e, il=il, hl=hl, i=i: e.matmul(
                        ps_oz[il][:, 0:129],
                        varq_row[0:1, hl * T + i * 128: hl * T + i * 128 + 128],
                        ecol[:], start=False, stop=True, skip_group_check=True),
                        s_pe)
                    oz_stop_pv[il] = pv
                    return pv

                def emit_post_dve(il, hl=hl, c=c):
                    pv = oz_stop_pv[il]
                    pr = il
                    em.wait("vector", s_pe, pv)
                    em.wait("vector", *gtok("zt"))
                    dv = em.op("vector", lambda e, il=il: e.tensor_scalar_add(
                        zt[:], ps_oz[il][:, 128:129], EPS * EPS * HD), s_dve)
                    em.wait("vector", s_dve, dv)
                    dv = em.op("vector", lambda e: e.reciprocal(rec[:], zt[:]), s_dve)
                    tok["zt"] = (s_dve, dv)
                    em.wait("vector", s_dve, dv)
                    em.wait("vector", *gtok(f"onat{pr}"))
                    dv = em.op("vector", lambda e, il=il, pr=pr: e.tensor_scalar_mul(
                        onat[pr][:], ps_oz[il][:, 0:128], rec[:]), s_dve)
                    tok[f"pb{ozbank[il]}"] = (s_dve, dv)
                    post_state[(hl, c, il)] = dv

                def emit_post_tr(il, hl=hl, c=c, g=g):
                    i = 4 * c + il
                    pr = il
                    bk = RBL[rbc[0] % 4]
                    rbc[0] += 1
                    em.wait("tensor", s_dve, post_state[(hl, c, il)])
                    em.wait("tensor", *gtok(f"pb{bk}"))
                    pv = em.op("tensor", lambda e, pr=pr, bk=bk: e.matmul(
                        pbk[bk][:, 0:128], onat[pr][:], ident[:],
                        start=True, stop=True), s_pe)
                    tok[f"onat{pr}"] = (s_pe, pv)
                    em.wait("scalar", s_pe, pv)
                    if c == 0 and il == 0:
                        em.wait("scalar", *gtok("oTgS"))
                        em.wait("scalar", *gtok("oTgS2"))
                    av = em.op("scalar", lambda e, i=i, bk=bk: e.activation(
                        oTg[:, i * 128: i * 128 + 128],
                        pbk[bk][:, 0:128], AF.Copy), s_act)
                    tok[f"pb{bk}"] = (s_act, av)

                for j in range(nj):
                    if j >= 2 and pending_post:
                        pending_post.pop(0)()
                    emit_dots(j)
                    if j >= 2:
                        emit_ozb(j - 2)
                emit_ozb(nj - 2)
                emit_ozb(nj - 1)
                for il in range(4):
                    pv_grp = emit_stop(il)
                for il in range(4):
                    emit_post_dve(il)
                pending_post[:] = [
                    (lambda il=il, f=emit_post_tr: f(il)) for il in range(4)]
                tok["grp_free"] = (s_pe, pv_grp)
            for fn in pending_post:
                fn()
            pending_post[:] = []
            em.wait("sync", s_act, s_act.n)
            em.op("sync", lambda e, g=g, hl=hl: e.dma_start(
                out=oT_scr[(g * GH + hl) * 128:(g * GH + hl) * 128 + 128, :],
                in_=oTg[:]), s_osp2[0], 16)
            tok["oTgS"] = (s_osp2[0], 0)
            if dbg and g == 0 and hl == 0:
                em.op("sync", lambda e: e.dma_start(
                    out=dbg_d["oTg"], in_=oTg[:]), s_dbg2, 16)
                tok["oTgS2"] = (s_dbg2, 0)
        if g < NG - 1:
            emit_wload_g(g + 1, 0)
            emit_wload_g(g + 1, 1)
            for hf in (0, 1):
                emit_wlo_half(g + 1, 0, hf)
                emit_wlo_half(g + 1, 1, hf)
            for e_ in ("tensor", "vector", "scalar"):
                em.wait(e_, *gtok("grp_free"))
        else:
            barrier()
        if stage == "attn" and g == 0:
            break

    if stage in ("qkv", "attn"):
        em.wait("sync", s_dbg2, 0)
        em.wait("sync", s_osp2[0], 0)
        em.wait("sync", s_osp2[1], 0)
        blk = nc.Block().__enter__()
        em.emit(blk)
        blk.__exit__(None, None, None)
        gc.close()
        ctx.close()
        return nc
    gc.close()

    woc = ExitStack()
    wsb = lambda name, shape, dt: woc.enter_context(nc.sbuf_tensor(name, shape, dt))
    wps = lambda name, shape: woc.enter_context(nc.psum_tensor(name, shape, F32))
    wo_sb = wsb("wo_sb", [128, 16 * 2048], BF16)
    oTsl = [wsb(f"oTsl_{i}", [128, 16 * 512], BF16) for i in range(2)]
    accst = [wsb(f"accst_{i}", [128, 2048], F32) for i in range(2)]
    ps_ao = [wps(f"ps_ao_{i}", [128, 512]) for i in range(4)]
    s_wo = mksem("s_wo", True)
    s_otl = [mksem(f"s_otl{i}", True) for i in range(2)]
    s_xr = [mksem(f"s_xr{i}", True) for i in range(2)]
    s_acst = [mksem(f"s_acst{i}", True) for i in range(2)]

    for ec in range(16):
        em.op("gpsimd", lambda e, ec=ec: e.dma_start(
            out=wo_sb[:, ec * 2048:(ec + 1) * 2048],
            in_=wo_d[ec * 128:(ec + 1) * 128, :]), s_wo, 16)
    oT_view = oT_scr.rearrange("(ec p) t -> p ec t", p=128)
    for tc in range(NTC):
        ob = tc % 2
        em.wait("sync", *gtok(f"oTsl{ob}"))
        for hf in range(2):
            em.op("sync", lambda e, ob=ob, tc=tc, hf=hf: e.dma_start(
                out=oTsl[ob][:].rearrange("p (ec t) -> p ec t", ec=16)
                [:, hf * 8:(hf + 1) * 8, :],
                in_=oT_view[:, hf * 8:(hf + 1) * 8, tc * 512:(tc + 1) * 512]),
                s_otl[ob], 16)
        pv_last = None
        for ttl in range(4):
            tt = tc * 4 + ttl
            xb = tt % 2
            em.wait("sync", *gtok(f"accst{xb}"))
            em.wait("sync", *gtok(f"xres{xb}"))
            em.op("sync", lambda e, xb=xb, tt=tt: e.dma_start(
                out=accst[xb][:], in_=x_d[tt * 128:(tt + 1) * 128, :]), s_xr[xb], 16)
            for ddc in range(4):
                pb = ddc % 4
                em.wait("tensor", s_wo, 0)
                em.wait("tensor", s_otl[ob], 0)
                em.wait("tensor", *gtok(f"ps_ao{pb}"))
                pv = None
                for ec in range(16):
                    pv = em.op("tensor", lambda e, pb=pb, ob=ob, ec=ec, ttl=ttl, ddc=ddc: e.matmul(
                        ps_ao[pb][:], oTsl[ob][:, ec * 512 + ttl * 128: ec * 512 + ttl * 128 + 128],
                        wo_sb[:, ec * 2048 + ddc * 512: ec * 2048 + ddc * 512 + 512],
                        start=(ec == 0), stop=(ec == 15)), s_pe if ec == 15 else None)
                pv_last = pv
                em.wait("vector", s_pe, pv)
                em.wait("vector", s_xr[xb], 0)
                dv = em.op("vector", lambda e, pb=pb, xb=xb, ddc=ddc: e.tensor_add(
                    accst[xb][:, ddc * 512:(ddc + 1) * 512], ps_ao[pb][:],
                    accst[xb][:, ddc * 512:(ddc + 1) * 512]), s_dve)
                tok[f"ps_ao{pb}"] = (s_dve, dv)
                if ddc == 3:
                    tok[f"xres{xb}"] = (s_dve, dv)
            em.wait("sync", s_dve, s_dve.n)
            em.op("sync", lambda e, xb=xb, tt=tt: e.dma_start(
                out=acc_scr[tt * 128:(tt + 1) * 128, :], in_=accst[xb][:]), s_acst[xb], 16)
            tok[f"accst{xb}"] = (s_acst[xb], 0)
        tok[f"oTsl{ob}"] = (s_pe, pv_last)
    barrier()
    woc.close()
    if stage == "wo":
        blk = nc.Block().__enter__()
        em.emit(blk)
        blk.__exit__(None, None, None)
        ctx.close()
        return nc

    fc = ExitStack()
    fsb = lambda name, shape, dt: fc.enter_context(nc.sbuf_tensor(name, shape, dt))
    fps = lambda name, shape: fc.enter_context(nc.psum_tensor(name, shape, F32))
    w1sl = [fsb(f"w1sl_{i}", [128, 2048], BF16) for i in range(2)]
    w3sl = [fsb(f"w3sl_{i}", [128, 2048], BF16) for i in range(2)]
    silust = [fsb(f"silust_{i}", [128, 512], F32) for i in range(2)]
    hidst = [fsb(f"hidst_{i}", [128, 512], BF16) for i in range(2)]
    ust = [fsb(f"ust_{i}", [128, 512], BF16) for i in range(2)]
    ps_g = [fps(f"ps_g_{i}", [128, 512]) for i in range(2)]
    ps_u3 = [fps(f"ps_u3_{i}", [128, 512]) for i in range(2)]
    ps_uh = [fps(f"ps_uh_{i}", [128, 512]) for i in range(2)]
    s_wA = [mksem(f"s_wA{i}", True) for i in range(2)]
    s_ust = [mksem(f"s_ust{i}", True) for i in range(2)]

    def emit_wAload(blk_i):
        bb = blk_i % 2
        em.wait("gpsimd", *gtok(f"w1sl{bb}"))
        em.op("gpsimd", lambda e, bb=bb, blk_i=blk_i: e.dma_start(
            out=w1sl[bb][:], in_=w1_d[blk_i]), s_wA[bb], 16)
        em.op("gpsimd", lambda e, bb=bb, blk_i=blk_i: e.dma_start(
            out=w3sl[bb][:], in_=w3_d[blk_i]), s_wA[bb], 16)

    emit_wAload(0)
    up_pv = {}

    def emit_ffn_up(st):
        blk_i, tc = divmod(st, NTC)
        bb = blk_i % 2
        cb = tc % 2
        em.wait("tensor", s_wA[bb], 0)
        em.wait("tensor", *gtok(f"ps_g{cb}"))
        for dc in range(NDC):
            em.op("tensor", lambda e, cb=cb, bb=bb, dc=dc, tc=tc: e.matmul(
                ps_g[cb][:], w1sl[bb][:, dc * 128:(dc + 1) * 128],
                xnT[:, dc * T + tc * 512: dc * T + tc * 512 + 512],
                start=(dc == 0), stop=(dc == NDC - 1)),
                s_pe if dc == NDC - 1 else None)
        em.wait("tensor", *gtok(f"ps_u3{cb}"))
        pv3 = None
        for dc in range(NDC):
            pv3 = em.op("tensor", lambda e, cb=cb, bb=bb, dc=dc, tc=tc: e.matmul(
                ps_u3[cb][:], w3sl[bb][:, dc * 128:(dc + 1) * 128],
                xnT[:, dc * T + tc * 512: dc * T + tc * 512 + 512],
                start=(dc == 0), stop=(dc == NDC - 1)),
                s_pe if dc == NDC - 1 else None)
        up_pv[st] = pv3
        if tc == NTC - 1:
            tok[f"w1sl{bb}"] = (s_pe, pv3)

    def emit_ffn_epi(st):
        blk_i, tc = divmod(st, NTC)
        cb = tc % 2
        pv3 = up_pv[st]
        em.wait("scalar", s_pe, pv3)
        em.wait("scalar", *gtok(f"silust{cb}"))
        av = em.op("scalar", lambda e, cb=cb: e.activation(
            silust[cb][:], ps_g[cb][:], AF.Silu), s_act)
        tok[f"ps_g{cb}"] = (s_act, av)
        em.wait("vector", s_act, av)
        em.wait("vector", *gtok(f"hidst{cb}"))
        dv = em.op("vector", lambda e, cb=cb: e.tensor_mul(
            hidst[cb][:], silust[cb][:], ps_u3[cb][:]), s_dve)
        tok[f"ps_u3{cb}"] = (s_dve, dv)
        tok[f"silust{cb}"] = (s_dve, dv)
        em.wait("tensor", s_dve, dv)
        em.wait("tensor", *gtok(f"ps_uh{cb}"))
        pv = em.op("tensor", lambda e, cb=cb: e.matmul(
            ps_uh[cb][:], h128[:], hidst[cb][:], start=True, stop=True), s_pe)
        tok[f"hidst{cb}"] = (s_pe, pv)
        em.wait("scalar", s_pe, pv)
        em.wait("scalar", *gtok(f"ust{cb}"))
        av = em.op("scalar", lambda e, cb=cb: e.activation(
            ust[cb][:], ps_uh[cb][:], AF.Copy), s_act)
        tok[f"ps_uh{cb}"] = (s_act, av)
        em.wait("sync", s_act, av)
        em.op("sync", lambda e, cb=cb, blk_i=blk_i, tc=tc: e.dma_start(
            out=u_scr[blk_i * 128:(blk_i + 1) * 128, tc * 512:(tc + 1) * 512],
            in_=ust[cb][:]), s_ust[cb], 16)
        tok[f"ust{cb}"] = (s_ust[cb], 0)

    for st in range(NHB * NTC):
        blk_i, tc = divmod(st, NTC)
        if tc == 0 and blk_i + 1 < NHB:
            emit_wAload(blk_i + 1)
        emit_ffn_up(st)
        if st >= 1:
            emit_ffn_epi(st - 1)
    emit_ffn_epi(NHB * NTC - 1)
    barrier()
    fc.close()
    if stage == "ffna":
        blk = nc.Block().__enter__()
        em.emit(blk)
        blk.__exit__(None, None, None)
        ctx.close()
        return nc

    bc = ExitStack()
    bsb = lambda name, shape, dt: bc.enter_context(nc.sbuf_tensor(name, shape, dt))
    bps = lambda name, shape: bc.enter_context(nc.psum_tensor(name, shape, F32))
    u_sl = bsb("u_sl", [128, NHB * 512], BF16)
    acc_sl = bsb("acc_sl", [128, 4 * 1024], F32)
    w2sl = [bsb(f"w2sl_{i}", [128, 2048], BF16) for i in range(3)]
    outst = [bsb(f"outst_{i}", [128, 1024], F32) for i in range(4)]
    ps_o = [bps(f"ps_o_{i}", [128, 512]) for i in range(8)]
    s_usl = mksem("s_usl", True)
    s_accl = mksem("s_accl", True)
    s_w2s = [mksem(f"s_w2s{i}", True) for i in range(3)]
    s_uslB = mksem("s_uslB", True)
    s_outB = mksem("s_outB", True)
    u_view = u_scr.rearrange("(hb p) t -> p hb t", p=128)
    acc_view = acc_scr.rearrange("(tcg ttl p) d -> tcg p ttl d", ttl=4, p=128)

    def emit_uload(tc, half):
        em.wait("sync", *gtok("u_slA" if half == 0 else "u_slB"))
        for part in range(4 * half, 4 * half + 4):
            em.op("sync", lambda e, tc=tc, part=part: e.dma_start(
                out=u_sl[:].rearrange("p (hb t) -> p hb t", hb=NHB)
                [:, part * 8:(part + 1) * 8, :],
                in_=u_view[:, part * 8:(part + 1) * 8, tc * 512:(tc + 1) * 512]),
                s_usl if half == 0 else s_uslB, 16)

    emit_uload(0, 0)
    emit_uload(0, 1)
    for tc in range(NTC):
        dv_last = None
        NQ = NHB // 2

        def emit_w2sload(qd, ddh):
            wb = (ddh * NQ + qd) % 3
            em.wait("scalar", *gtok(f"w2sl{wb}"))
            em.op("scalar", lambda e, wb=wb, qd=qd, ddh=ddh: e.dma_start(
                out=w2sl[wb][:].rearrange("p (b d) -> p b d", b=2),
                in_=w2pp_scr.rearrange("(b p) d -> p b d", p=128)
                [:, qd * 2:(qd + 1) * 2, ddh * 1024:(ddh + 1) * 1024]),
                s_w2s[wb], 16)

        for ddh in range(2):
            em.wait("sync", *gtok("acc_sl"))
            em.op("sync", lambda e, tc=tc, ddh=ddh: e.dma_start(
                out=acc_sl[:].rearrange("p (ttl d) -> p ttl d", ttl=4),
                in_=acc_view[tc][:, :, ddh * 1024:(ddh + 1) * 1024]), s_accl, 16)
            pv_fin = None
            if ddh == 0:
                emit_w2sload(0, 0)
                emit_w2sload(1, 0)
            for qd in range(NQ):
                wb = (ddh * NQ + qd) % 3
                if qd + 2 < NQ:
                    emit_w2sload(qd + 2, ddh)
                elif ddh == 0:
                    emit_w2sload(qd + 2 - NQ, 1)
                em.wait("tensor", s_w2s[wb], 0)
                if qd == 0:
                    em.wait("tensor", s_usl, 0)
                    for k in range(8):
                        em.wait("tensor", *gtok(f"ps_o{k}"))
                elif qd == NQ // 2:
                    em.wait("tensor", s_uslB, 0)
                pv = None
                for hq in range(2):
                    hb = qd * 2 + hq
                    for ttl in range(4):
                        for dd5 in range(2):
                            pv = em.op("tensor", lambda e, ttl=ttl, dd5=dd5, wb=wb, hb=hb, hq=hq: e.matmul(
                                ps_o[ttl * 2 + dd5][:],
                                u_sl[:, hb * 512 + ttl * 128: hb * 512 + ttl * 128 + 128],
                                w2sl[wb][:, hq * 1024 + dd5 * 512: hq * 1024 + dd5 * 512 + 512],
                                start=(hb == 0), stop=(hb == NHB - 1),
                                skip_group_check=True), s_pe)
                tok[f"w2sl{wb}"] = (s_pe, pv)
                pv_fin = pv
                if ddh == 1 and qd == NQ // 2 - 1:
                    tok["u_slA"] = (s_pe, pv)
                    if tc + 1 < NTC:
                        emit_uload(tc + 1, 0)
            if ddh == 1:
                tok["u_slB"] = (s_pe, pv_fin)
                if tc + 1 < NTC:
                    emit_uload(tc + 1, 1)
            for ttl in range(4):
                ttl_dv = None
                for dd5 in range(2):
                    k = ttl * 2 + dd5
                    em.wait("vector", s_pe, pv_fin)
                    em.wait("vector", s_accl, 0)
                    if dd5 == 0:
                        em.wait("vector", *gtok(f"outst{ttl}"))
                    dv = em.op("vector", lambda e, k=k, ttl=ttl, dd5=dd5: e.tensor_add(
                        outst[ttl][:, dd5 * 512: dd5 * 512 + 512],
                        ps_o[k][:],
                        acc_sl[:, ttl * 1024 + dd5 * 512:
                               ttl * 1024 + dd5 * 512 + 512]), s_dve)
                    tok[f"ps_o{k}"] = (s_dve, dv)
                    ttl_dv = dv
                    dv_last = dv
                tt = tc * 4 + ttl
                em.wait("sync", s_dve, ttl_dv)
                em.op("sync", lambda e, ttl=ttl, tt=tt, ddh=ddh: e.dma_start(
                    out=out_d[tt * 128:(tt + 1) * 128, ddh * 1024:(ddh + 1) * 1024],
                    in_=outst[ttl][:]), s_outB, 16)
                tok[f"outst{ttl}"] = (s_outB, 0)
            tok["acc_sl"] = (s_dve, dv_last)

    em.wait("sync", s_outB, 0)
    if dbg:
        em.wait("sync", s_dbg, 0)
        em.wait("sync", s_dbg2, 0)
    blk = nc.Block().__enter__()
    em.emit(blk)
    blk.__exit__(None, None, None)
    bc.close()
    ctx.close()
    return nc


def hmat(n):
    h = np.array([[1.0]], dtype=np.float64)
    while h.shape[0] < n:
        h = np.block([[h, h], [h, -h]])
    return h


def prep_inputs(inputs):
    x = np.ascontiguousarray(np.asarray(inputs["L0_attn_hidden_states"]))
    Wq = np.asarray(inputs["L0_attn_Wq"]); Wk = np.asarray(inputs["L0_attn_Wk"])
    Wv = np.asarray(inputs["L0_attn_Wv"]); Wo = np.asarray(inputs["L0_attn_Wo"])
    gamma = np.asarray(inputs["L0_attn_gamma"]); beta = np.asarray(inputs["L0_attn_beta"])
    W1 = np.asarray(inputs["L0_ffn_W1"]); W2 = np.asarray(inputs["L0_ffn_W2"])
    W3 = np.asarray(inputs["L0_ffn_W3"])

    bf = ml_dtypes.bfloat16
    blk16 = lambda W: np.ascontiguousarray(
        W.reshape(16, 128, 16, 128).transpose(0, 3, 2, 1)).reshape(16, 128, 2048)
    wq = blk16(Wq)
    wk = blk16(Wk)
    wv = np.ascontiguousarray(
        Wv.reshape(4, 512, 16, 128).transpose(0, 3, 2, 1)).reshape(4, 128, 8192)
    wo = np.ascontiguousarray(Wo.T)
    blk64 = lambda W: np.ascontiguousarray(
        W.reshape(64, 128, 16, 128).transpose(0, 3, 2, 1)).reshape(64, 128, 2048)
    w1 = blk64(W1)
    w3 = blk64(W3)
    w2T = np.ascontiguousarray(W2.T)
    perm = np.arange(NH)
    perm = (perm % 64) * 128 + (perm // 64)
    w2p = np.ascontiguousarray(w2T[perm])

    consts = {
        "ident": np.eye(128).astype(bf),
        "h128": hmat(128).astype(bf),
        "h64d": np.kron(np.eye(2), hmat(64)).astype(bf),
        "oneD": np.full((128, 1), 1.0 / 128.0).astype(bf),
        "onedm": np.full((128, 128), 1.0 / 128.0).astype(bf),
        "dmask": np.concatenate(
            [np.concatenate([np.triu(np.ones((128, 128))) if il == jl else
                             np.ones((128, 128)) for il in range(4)], axis=1)
             for jl in range(4)], axis=1).astype(bf),
        "onesr": np.ones((1, 128)).astype(bf),
        "one11": np.ones((1, 1)).astype(bf),
        "ecol": np.concatenate([np.zeros(128), [2.0 ** -10]]).reshape(1, 129).astype(bf),
        "gamma": np.ascontiguousarray(np.asarray(gamma, np.float32).reshape(128, 1)),
        "beta": np.ascontiguousarray(np.asarray(beta, np.float32).reshape(128, 1)),
    }
    shared = dict(wq=wq, wk=wk, wv=wv, wo=wo, w1=w1, w3=w3, w2p=w2p, **consts)
    in_maps = []
    for b in range(x.shape[0]):
        m = dict(shared)
        m["x"] = np.ascontiguousarray(x[b])
        in_maps.append(m)
    return in_maps


from concourse.bass_utils import run_bass_kernel_spmd


def kernel(**inputs):
    nc = build_program(stage="full", dbg=False)
    in_maps = prep_inputs(inputs)
    res = run_bass_kernel_spmd(nc, in_maps, core_ids=list(range(8)))
    return np.stack([np.asarray(r["out"], dtype=np.float32) for r in res.results],
                    axis=0)


# version 22
# speedup vs baseline: 1.1067x; 1.0034x over previous
import numpy as np
import ml_dtypes
from contextlib import ExitStack

import concourse.bass as bass
import concourse.mybir as mybir

F32 = mybir.dt.float32
BF16 = mybir.dt.bfloat16
AF = mybir.ActivationFunctionType
ALU = mybir.AluOpType
AX = mybir.AxisListType

EPS = 1e-5
T = 2048
D = 2048
NH = 8192
HD = 128
NHEADS = 16
NG = 4
GH = 4
NTT = 16
NTC = 4
NDC = 16
NHB = 64
RSQ = float(1.0 / np.sqrt(NH))

ENGINES = ("sync", "gpsimd", "tensor", "vector", "scalar")


class Sem:
    __slots__ = ("h", "n", "name", "is_dma")

    def __init__(self, ctx, nc, name, is_dma=False):
        self.h = ctx.enter_context(nc.semaphore(name))
        self.n = 0
        self.name = name
        self.is_dma = is_dma


class Em:

    def __init__(self):
        self.q = {e: [] for e in ENGINES}
        self.own = {}

    def wait(self, eng, sem, val):
        if sem.is_dma:
            val = sem.n
        if val > 0:
            q = self.q[eng]
            if q and q[-1][0] == "w" and q[-1][1] is sem:
                if q[-1][2] >= val:
                    return
                q.pop()
            q.append(("w", sem, val))

    def waitsnap(self, eng, snapshot):
        for sem, val in snapshot:
            self.wait(eng, sem, val)

    def op(self, eng, fn, inc=None, k=1):
        own = self.own.get(eng)
        if own is not None:
            self.wait(eng, own, own.n)
            if inc is None:
                inc = own
        if inc is None:
            self.q[eng].append(("o", fn, None, 0))
            return None
        inc.n += k
        self.q[eng].append(("o", fn, inc, k))
        return inc.n

    def emit(self, block):
        for ename in ENGINES:
            items = self.q[ename]

            def run(eng, items=items):
                for it in items:
                    if it[0] == "w":
                        eng.wait_ge(it[1].h, it[2])
                    else:
                        instr = it[1](eng)
                        if it[2] is not None:
                            instr.then_inc(it[2].h, it[3])
            getattr(block, ename)(run)


def build_program(stage="full", dbg=False):
    nc = bass.Bass("TRN2", target_bir_lowering=False, debug=False)
    scr_kind = "ExternalOutput" if dbg else "Internal"

    dr = lambda name, shape, dt, kind: nc.dram_tensor(name, shape, dt, kind=kind).ap()
    x_d = dr("x", [T, D], F32, "ExternalInput")
    wq_d = dr("wq", [16, 128, 2048], F32, "ExternalInput")
    wk_d = dr("wk", [16, 128, 2048], F32, "ExternalInput")
    wv_d = dr("wv", [4, 128, 8192], F32, "ExternalInput")
    wo_d = dr("wo", [2048, 2048], F32, "ExternalInput")
    w1_d = dr("w1", [64, 128, 2048], F32, "ExternalInput")
    w3_d = dr("w3", [64, 128, 2048], F32, "ExternalInput")
    w2p_d = dr("w2p", [8192, 2048], F32, "ExternalInput")
    gamma_d = dr("gamma", [128, 1], F32, "ExternalInput")
    beta_d = dr("beta", [128, 1], F32, "ExternalInput")
    ident_d = dr("ident", [128, 128], BF16, "ExternalInput")
    h128_d = dr("h128", [128, 128], BF16, "ExternalInput")
    h64d_d = dr("h64d", [128, 128], BF16, "ExternalInput")
    oneD_d = dr("oneD", [128, 1], BF16, "ExternalInput")
    onedm_d = dr("onedm", [128, 128], BF16, "ExternalInput")
    dmask_d = dr("dmask", [128, 4 * 512], BF16, "ExternalInput")
    onesr_d = dr("onesr", [1, 128], BF16, "ExternalInput")
    one11_d = dr("one11", [1, 1], BF16, "ExternalInput")
    ecol_d = dr("ecol", [1, 129], BF16, "ExternalInput")

    out_d = dr("out", [T, D], F32, "ExternalOutput")
    u_scr = dr("u_scr", [NH, T], BF16, scr_kind)
    w2pp_scr = dr("w2pp_scr", [NH, D], BF16, scr_kind)
    oT_scr = dr("oT_scr", [D, T], BF16, scr_kind)
    acc_scr = dr("acc_scr", [T, D], F32, scr_kind)
    dbg_d = {}
    if dbg:
        dbg_d["xnT"] = dr("dbg_xnT", [128, NDC * T], BF16, "ExternalOutput")
        dbg_d["ybq"] = dr("dbg_ybq", [128, GH * T], BF16, "ExternalOutput")
        dbg_d["ybk"] = dr("dbg_ybk", [128, GH * T], BF16, "ExternalOutput")
        dbg_d["vaug"] = dr("dbg_vaug", [128, NTT * GH * 129], BF16, "ExternalOutput")
        dbg_d["invk"] = dr("dbg_invk", [128, 64], F32, "ExternalOutput")
        dbg_d["varq"] = dr("dbg_varq", [1, GH * T], BF16, "ExternalOutput")
        dbg_d["oTg"] = dr("dbg_oTg", [128, T], BF16, "ExternalOutput")

    ctx = ExitStack()
    sb = lambda name, shape, dt: ctx.enter_context(nc.sbuf_tensor(name, shape, dt))

    xnT = sb("xnT", [128, NDC * T], BF16)
    xnloT = sb("xnloT", [128, NDC * 128], BF16)
    wslab = [sb(f"wslab_{i}", [128, 2048], BF16) for i in range(3)]
    vslab = sb("vslab", [128, 4096], BF16)
    ident = sb("ident_s", [128, 128], BF16)
    h128 = sb("h128_s", [128, 128], BF16)
    h64d = sb("h64d_s", [128, 128], BF16)
    oneD = sb("oneD_s", [128, 1], BF16)
    onedm = sb("onedm_s", [128, 128], BF16)
    onesr = sb("onesr_s", [1, 128], BF16)
    one11 = sb("one11_s", [1, 1], BF16)
    ecol = sb("ecol_s", [1, 129], BF16)
    gam = sb("gam_s", [128, 1], F32)
    bet = sb("bet_s", [128, 1], F32)
    betc = sb("betc_s", [128, 1], F32)
    betbf = sb("betbf_s", [128, 1], BF16)
    meanb = sb("meanb_s", [1, 1], BF16)

    em = Em()
    ALL = []

    def mksem(name, is_dma=False):
        s_ = Sem(ctx, nc, name, is_dma)
        ALL.append(s_)
        return s_

    s_pe = mksem("s_pe")
    s_dve = mksem("s_dve")
    s_act = mksem("s_act")
    em.own["vector"] = s_dve
    em.own["scalar"] = s_act

    def snap():
        return [(s, s.n) for s in ALL]

    def barrier():
        sn = snap()
        for e in ENGINES:
            em.waitsnap(e, sn)

    s_const = mksem("s_const", True)
    for dst, src in ((ident, ident_d), (h128, h128_d), (h64d, h64d_d),
                     (oneD, oneD_d), (onedm, onedm_d), (onesr, onesr_d),
                     (one11, one11_d), (ecol, ecol_d), (gam, gamma_d), (bet, beta_d)):
        em.op("sync", lambda e, d=dst, s=src: e.dma_start(out=d[:], in_=s), s_const, 16)
    barrier()

    tok = {}

    def gtok(name):
        return tok.get(name, (s_pe, 0))

    lnc = ExitStack()
    lsb = lambda name, shape, dt: lnc.enter_context(nc.sbuf_tensor(name, shape, dt))
    lps = lambda name, shape: lnc.enter_context(nc.psum_tensor(name, shape, F32))
    xt32 = [lsb(f"xt32_{i}", [128, 2048], F32) for i in range(2)]
    xsqt = lsb("xsqt", [128, 2048], F32)
    xn32t = lsb("xn32t", [128, 2048], F32)
    xnlon = lsb("xnlon", [128, 2048], BF16)
    xnbf = [lsb(f"xnbf_{i}", [128, 2048], BF16) for i in range(2)]
    st_c = {n: [lsb(f"ln_{n}_{i}", [128, 1], F32) for i in range(2)]
            for n in ("xsum", "xss", "mu", "var", "rstd", "nb")}
    ps_tr = [lps(f"ps_tr_{i}", [128, 512]) for i in range(2)]
    s_xt = [mksem(f"s_xt{i}", True) for i in range(2)]

    dv = em.op("vector", lambda e: e.tensor_copy(betbf[:], bet[:]), s_dve)
    em.wait("tensor", s_dve, dv)
    pv = em.op("tensor", lambda e: e.matmul(
        ps_tr[0][0:1, 0:1], oneD[:], betbf[:], start=True, stop=True), s_pe)
    em.wait("scalar", s_pe, pv)
    av = em.op("scalar", lambda e: e.activation(
        meanb[:], ps_tr[0][0:1, 0:1], AF.Copy), s_act)
    em.wait("tensor", s_act, av)
    pv = em.op("tensor", lambda e: e.matmul(
        ps_tr[0][:, 0:1], onesr[:], meanb[:], start=True, stop=True), s_pe)
    em.wait("vector", s_pe, pv)
    dv = em.op("vector", lambda e: e.tensor_sub(betc[:], bet[:], ps_tr[0][:, 0:1]), s_dve)

    tok_xt = [(s_dve, 0), (s_dve, 0)]
    tok_xn = [(s_pe, 0), (s_pe, 0)]
    tok_pstr = [(s_dve, dv), (s_dve, 0)]

    w2pb = [lsb(f"w2pb_{i}", [128, 2048], BF16) for i in range(4)]
    w2pb32 = [lsb(f"w2pb32_{i}", [128, 2048], F32) for i in range(2)]
    foldst = [lsb(f"foldst_{i}", [128, 2048], BF16) for i in range(2)]
    ps_f = [lps(f"ps_f_{i}", [128, 512]) for i in range(2)]
    s_w2p = [mksem(f"s_w2p{i}", True) for i in range(4)]
    s_w2p32 = [mksem(f"s_w2p32_{i}", True) for i in range(2)]
    s_fst = [mksem(f"s_fst{i}", True) for i in range(2)]
    w2pp_view = w2pp_scr.rearrange("(a bb) d -> bb a d", bb=128)

    def emit_w2pload(blk_i):
        bb = blk_i % 4
        if blk_i % 2 == 0:
            em.wait("gpsimd", *gtok(f"w2pb{bb}"))
            em.op("gpsimd", lambda e, bb=bb, blk_i=blk_i: e.dma_start(
                out=w2pb[bb][:], in_=w2p_d[blk_i * 128:(blk_i + 1) * 128, :]),
                s_w2p[bb], 16)
        else:
            st = (blk_i % 4) // 2
            em.wait("sync", *gtok(f"w2pb32_{st}"))
            em.op("sync", lambda e, st=st, blk_i=blk_i: e.dma_start(
                out=w2pb32[st][:], in_=w2p_d[blk_i * 128:(blk_i + 1) * 128, :]),
                s_w2p32[st], 16)
            em.wait("vector", s_w2p32[st], 0)
            em.wait("vector", *gtok(f"w2pb{bb}"))
            dv = em.op("vector", lambda e, bb=bb, st=st: e.tensor_copy(
                w2pb[bb][:], w2pb32[st][:]), s_dve)
            tok[f"w2pb32_{st}"] = (s_dve, dv)
            tok[f"w2pb{bb}_rdy"] = (s_dve, dv)

    def emit_fold_job(blk_i):
        bb = blk_i % 4
        fs = blk_i % 2
        if blk_i + 3 < NHB:
            emit_w2pload(blk_i + 3)
        dv_last = None
        pv_last = None
        for ddc in range(4):
            pb = ddc % 2
            if blk_i % 2 == 0:
                em.wait("tensor", s_w2p[bb], 0)
            else:
                em.wait("tensor", *gtok(f"w2pb{bb}_rdy"))
            em.wait("tensor", *gtok(f"ps_f{pb}"))
            pv = em.op("tensor", lambda e, pb=pb, bb=bb, ddc=ddc: e.matmul(
                ps_f[pb][:], h64d[:], w2pb[bb][:, ddc * 512:(ddc + 1) * 512],
                start=True, stop=True), s_pe)
            pv_last = pv
            em.wait("vector", s_pe, pv)
            if ddc == 0:
                em.wait("vector", *gtok(f"foldst{fs}"))
            dv = em.op("vector", lambda e, pb=pb, fs=fs, ddc=ddc: e.tensor_scalar_mul(
                foldst[fs][:, ddc * 512:(ddc + 1) * 512], ps_f[pb][:], RSQ), s_dve)
            tok[f"ps_f{pb}"] = (s_dve, dv)
            dv_last = dv
        tok[f"w2pb{bb}"] = (s_pe, pv_last)
        q = "scalar"
        em.wait(q, s_dve, dv_last)
        for bit in range(2):
            em.op(q, lambda e, blk_i=blk_i, bit=bit, fs=fs: e.dma_start(
                out=w2pp_view[2 * blk_i + bit],
                in_=foldst[fs][bit * 64:(bit + 1) * 64, :]), s_fst[fs], 16)
        tok[f"foldst{fs}"] = (s_fst[fs], 0)

    emit_w2pload(0)
    emit_w2pload(1)
    emit_w2pload(2)

    def emit_xload(j):
        jb = j % 2
        em.wait("scalar", *tok_xt[jb])
        em.op("scalar", lambda e, j=j, jb=jb: e.dma_start(
            out=xt32[jb][:], in_=x_d[j * 128:(j + 1) * 128, :]), s_xt[jb], 16)

    emit_xload(0)
    for j in range(NTT):
        if j + 1 < NTT:
            emit_xload(j + 1)
        for _fj in range(4):
            emit_fold_job(j * 4 + _fj)
        jb = j % 2
        em.wait("vector", s_xt[jb], 0)
        rv = em.op("vector", lambda e, jb=jb: e.reduce_sum(
            st_c["xsum"][jb][:], xt32[jb][:], axis=AX.X), s_dve)
        em.wait("scalar", s_xt[jb], 0)
        av = em.op("scalar", lambda e, jb=jb: e.activation(
            xsqt[:], xt32[jb][:], AF.Square, accum_out=st_c["xss"][jb][:]), s_act)
        em.wait("vector", s_act, av)
        em.wait("vector", s_dve, rv)
        em.op("vector", lambda e, jb=jb: e.tensor_scalar_mul(
            st_c["mu"][jb][:], st_c["xsum"][jb][:], 1.0 / 2048.0))
        em.op("vector", lambda e, jb=jb: e.tensor_scalar_mul(
            st_c["var"][jb][:], st_c["xss"][jb][:], 1.0 / 2048.0))
        dv = em.op("vector", lambda e, jb=jb: e.scalar_tensor_tensor(
            st_c["rstd"][jb][:], st_c["mu"][jb][:], -1.0, st_c["mu"][jb][:],
            ALU.mult, ALU.mult), s_dve)
        em.wait("vector", s_dve, dv)
        dv = em.op("vector", lambda e, jb=jb: e.tensor_add(
            st_c["var"][jb][:], st_c["var"][jb][:], st_c["rstd"][jb][:]), s_dve)
        em.wait("vector", s_dve, dv)
        dv = em.op("vector", lambda e, jb=jb: e.tensor_scalar_add(
            st_c["var"][jb][:], st_c["var"][jb][:], EPS), s_dve)
        em.wait("scalar", s_dve, dv)
        av = em.op("scalar", lambda e, jb=jb: e.activation(
            st_c["rstd"][jb][:], st_c["var"][jb][:], AF.Ln), s_act)
        em.wait("scalar", s_act, av)
        av = em.op("scalar", lambda e, jb=jb: e.activation(
            st_c["rstd"][jb][:], st_c["rstd"][jb][:], AF.Exp, scale=-0.5), s_act)
        em.wait("vector", s_act, av)
        dv = em.op("vector", lambda e, jb=jb: e.scalar_tensor_tensor(
            st_c["nb"][jb][:], st_c["mu"][jb][:], -1.0, st_c["rstd"][jb][:],
            ALU.mult, ALU.mult), s_dve)
        em.wait("vector", s_dve, dv)
        em.wait("vector", *tok_xn[jb])
        av = em.op("vector", lambda e, jb=jb: e.tensor_scalar(
            xnbf[jb][:], xt32[jb][:], st_c["rstd"][jb][:], st_c["nb"][jb][:],
            ALU.mult, ALU.add), s_dve)
        if j == 0:
            av = em.op("vector", lambda e, jb=jb: e.tensor_scalar(
                xn32t[:], xt32[jb][:], st_c["rstd"][jb][:], st_c["nb"][jb][:],
                ALU.mult, ALU.add), s_dve)
            em.wait("vector", s_dve, av)
            av = em.op("vector", lambda e, jb=jb: e.tensor_sub(
                xnlon[:], xn32t[:], xnbf[jb][:]), s_dve)
        tok_xt[jb] = (s_dve, av)
        for dq in range(4):
            qb = dq % 2
            em.wait("tensor", s_dve, av)
            em.wait("tensor", *tok_pstr[qb])
            pv = None
            for kk in range(4):
                dc = dq * 4 + kk
                pv = em.op("tensor", lambda e, jb=jb, qb=qb, dc=dc, kk=kk: e.matmul(
                    ps_tr[qb][:, kk * 128:(kk + 1) * 128],
                    xnbf[jb][:, dc * 128:(dc + 1) * 128], ident[:],
                    start=True, stop=True), s_pe)
            tok_xn[jb] = (s_pe, pv)
            em.wait("vector", s_pe, pv)
            dv = em.op("vector", lambda e, j=j, qb=qb, dq=dq: e.tensor_copy(
                xnT[:].rearrange("p (dc t) -> p dc t", dc=NDC)
                [:, dq * 4:(dq + 1) * 4, j * 128:(j + 1) * 128],
                ps_tr[qb][:].rearrange("p (k t) -> p k t", k=4)), s_dve)
            tok_pstr[qb] = (s_dve, dv)
        if j == 0:
            for dq in range(4):
                qb = dq % 2
                em.wait("tensor", s_dve, av)
                em.wait("tensor", *tok_pstr[qb])
                pv = None
                for kk in range(4):
                    dc = dq * 4 + kk
                    pv = em.op("tensor", lambda e, qb=qb, dc=dc, kk=kk: e.matmul(
                        ps_tr[qb][:, kk * 128:(kk + 1) * 128],
                        xnlon[:, dc * 128:(dc + 1) * 128], ident[:],
                        start=True, stop=True), s_pe)
                em.wait("vector", s_pe, pv)
                dv = em.op("vector", lambda e, qb=qb, dq=dq: e.tensor_copy(
                    xnloT[:].rearrange("p (dc t) -> p dc t", dc=NDC)
                    [:, dq * 4:(dq + 1) * 4, :],
                    ps_tr[qb][:].rearrange("p (k t) -> p k t", k=4)), s_dve)
                tok_pstr[qb] = (s_dve, dv)

    sn_ln = snap()
    for e_ in ("tensor", "vector", "scalar", "sync"):
        em.waitsnap(e_, sn_ln)
    s_dbg = mksem("s_dbg", True)
    if dbg:
        em.op("sync", lambda e: e.dma_start(out=dbg_d["xnT"], in_=xnT[:]), s_dbg, 16)
    if stage == "ln":
        em.wait("sync", s_dbg, 0)
        blk = nc.Block().__enter__()
        em.emit(blk)
        blk.__exit__(None, None, None)
        lnc.close(); ctx.close()
        return nc
    lnc.close()

    gc = ExitStack()
    gsb = lambda name, shape, dt: gc.enter_context(nc.sbuf_tensor(name, shape, dt))
    gps = lambda name, shape: gc.enter_context(nc.psum_tensor(name, shape, F32))
    ybq = gsb("ybq", [128, GH * T], BF16)
    ybk = gsb("ybk", [128, GH * T], BF16)
    vaug = gsb("vaug", [128, NTT * GH * 129], BF16)
    attnS = [gsb(f"attnS_{i}", [128, 512], BF16) for i in range(3)]
    oTg = gsb("oTg_0", [128, T], BF16)
    varq_row = gsb("varq_row", [1, GH * T], BF16)
    vark_row = gsb("vark_row", [1, GH * T], BF16)
    invk_cols = gsb("invk_cols", [128, 64], F32)
    dmask = gsb("dmask_s", [128, 128], BF16)
    em.op("sync", lambda e: e.dma_start(out=dmask[:], in_=dmask_d[:, 0:128]), s_const, 16)
    invk_tmp = gsb("invk_tmp", [128, 16], F32)

    ych = [gsb(f"ych_{i}", [128, 512], BF16) for i in range(2)]
    y2ch = ych
    zt = gsb("zt", [128, 1], F32)
    rec = gsb("rec", [128, 1], F32)
    onat = [gsb(f"onat_{i}", [128, 128], BF16) for i in range(4)]
    yq0f = gsb("yq0f", [128, GH * 128], F32)
    ych0f = gsb("ych0f", [128, 128], F32)
    wlo = [gsb(f"wlo_{i}", [128, 2048], BF16) for i in range(3)]
    wslab32 = gsb("wslab32", [128, 1024], F32)
    corrf = gsb("corrf", [128, 128], F32)
    mu0f = gsb("mu0f", [128, 128], F32)
    yk0f = gsb("yk0f", [128, GH * 128], F32)
    attn0b = gsb("attn0b", [128, 128], BF16)
    rbc = [0]
    pending_post = []
    pbk = [gps(f"pb_{i}", [128, 512]) for i in range(8)]
    ps_y = [pbk[0], pbk[1]]
    ps_rb = [pbk[2], pbk[3], pbk[4]]
    ps_oz = {0: pbk[5], 1: pbk[6], 2: pbk[7], 3: pbk[0]}
    s_wsl = [mksem(f"s_wsl{i}", True) for i in range(3)]
    s_w32 = mksem("s_w32", True)
    s_vsl = mksem("s_vsl", True)
    s_osp2 = [mksem(f"s_osp{i}", True) for i in range(2)]
    s_dbg2 = mksem("s_dbg2", True)

    for g in range(NG):
        jobs = [("k", wk_d, ybk, vark_row, hl) for hl in range(GH)] + \
               [("q", wq_d, ybq, varq_row, hl) for hl in range(GH)]

        def emit_wload(idx, g=g, jobs=jobs):
            side, wsrc, ybuf, vrow, hl = jobs[idx]
            wb = idx % 3
            ec = g * GH + hl
            em.wait("gpsimd", *gtok(f"wslab{wb}"))
            em.op("gpsimd", lambda e, wb=wb, wsrc=wsrc, ec=ec: e.dma_start(
                out=wslab[wb][:], in_=wsrc[ec]), s_wsl[wb], 16)

        def emit_wload_g(gx, idx):
            wb = idx % 3
            wsrc = wk_d if idx < GH else wq_d
            ec = gx * GH + (idx % GH)
            em.wait("gpsimd", *gtok(f"wslab{wb}"))
            em.op("gpsimd", lambda e, wb=wb, wsrc=wsrc, ec=ec: e.dma_start(
                out=wslab[wb][:], in_=wsrc[ec]), s_wsl[wb], 16)

        def emit_wlo_half(gx, idx, hf):
            side = "k" if idx < GH else "q"
            hl_ = idx % GH
            wsrc = wk_d if side == "k" else wq_d
            w3 = idx % 3
            wb = idx % 3
            ec = gx * GH + hl_
            em.wait("scalar", *gtok("wslab32"))
            em.op("scalar", lambda e, wsrc=wsrc, ec=ec, hf=hf: e.dma_start(
                out=wslab32[:], in_=wsrc[ec][:, hf * 1024:(hf + 1) * 1024]),
                s_w32, 16)
            em.wait("vector", s_w32, 0)
            em.wait("vector", s_wsl[wb], 0)
            if hf == 0:
                em.wait("vector", *gtok(f"wlo{w3}"))
            dv = em.op("vector", lambda e, w3=w3, wb=wb, hf=hf: e.tensor_sub(
                wlo[w3][:, hf * 1024:(hf + 1) * 1024], wslab32[:],
                wslab[wb][:, hf * 1024:(hf + 1) * 1024]), s_dve)
            tok["wslab32"] = (s_dve, dv)
            if hf == 1:
                tok[f"wlo_rdy{w3}"] = (s_dve, dv)

        if g == 0:
            emit_wload(0)
            emit_wload(1)
        for _job_i in range(len(jobs)):
            if _job_i + 2 < len(jobs):
                emit_wload(_job_i + 2)
            side, wsrc, ybuf, vrow, hl = jobs[_job_i]
            if True:
                ec = g * GH + hl
                wb = _job_i % 3
                pv_last = None
                ymm_pv = {}

                def emit_ymm(tc, wb=wb):
                    cb = tc % 2
                    em.wait("tensor", s_wsl[wb], 0)
                    em.wait("tensor", *gtok(f"pb{cb}"))
                    pv = None
                    for dc in range(NDC):
                        pv = em.op("tensor", lambda e, cb=cb, wb=wb, dc=dc, tc=tc: e.matmul(
                            ps_y[cb][:], wslab[wb][:, dc * 128:(dc + 1) * 128],
                            xnT[:, dc * T + tc * 512: dc * T + tc * 512 + 512],
                            start=(dc == 0), stop=(dc == NDC - 1)),
                            s_pe if dc == NDC - 1 else None)
                    ymm_pv[tc] = pv
                    return pv

                sq_av = {}

                def emit_statsA(tc, side=side, ybuf=ybuf, vrow=vrow, hl=hl):
                    cb = tc % 2
                    pv = ymm_pv[tc]
                    em.wait("scalar", s_pe, pv)
                    em.wait("scalar", *gtok(f"ych{cb}"))
                    av = em.op("scalar", lambda e, cb=cb: e.activation(
                        ych[cb][:], ps_y[cb][:], AF.Copy, scale=gam[:]), s_act)
                    if tc == 0:
                        em.wait("scalar", *gtok("ych0f"))
                        av = em.op("scalar", lambda e, cb=cb: e.activation(
                            ych0f[:], ps_y[cb][:, 0:128], AF.Copy, scale=gam[:]), s_act)
                    tok[f"pb{cb}"] = (s_act, av)
                    em.wait("tensor", s_act, av)
                    em.wait("tensor", *gtok(f"pb{cb + 2}"))
                    pv = em.op("tensor", lambda e, cb=cb: e.matmul(
                        ps_rb[cb][:], onedm[:], ych[cb][:], start=True, stop=True), s_pe)
                    em.wait("vector", s_pe, pv)
                    dv = em.op("vector", lambda e, cb=cb, hl=hl, tc=tc, ybuf=ybuf: e.scalar_tensor_tensor(
                        ybuf[:, hl * T + tc * 512: hl * T + tc * 512 + 512],
                        ych[cb][:], betc[:], ps_rb[cb][:], ALU.add, ALU.subtract), s_dve)
                    if tc == 0:
                        dv = em.op("vector", lambda e, cb=cb: e.tensor_copy(
                            mu0f[:], ps_rb[cb][:, 0:128]), s_dve)
                    tok[f"ych{cb}"] = (s_dve, dv)
                    tok[f"pb{cb + 2}"] = (s_dve, dv)
                    em.wait("scalar", s_dve, dv)
                    em.wait("scalar", *gtok(f"y2ch{cb}"))
                    av = em.op("scalar", lambda e, cb=cb, hl=hl, tc=tc, ybuf=ybuf: e.activation(
                        y2ch[cb][:], ybuf[:, hl * T + tc * 512: hl * T + tc * 512 + 512],
                        AF.Square), s_act)
                    sq_av[tc] = av

                def emit_statsB(tc, side=side, ybuf=ybuf, vrow=vrow, hl=hl):
                    cb = tc % 2
                    em.wait("tensor", s_act, sq_av[tc])
                    em.wait("tensor", *gtok(f"pb{cb + 2}"))
                    pv = em.op("tensor", lambda e, cb=cb: e.matmul(
                        ps_rb[cb][:], onedm[:], y2ch[cb][:], start=True, stop=True), s_pe)
                    tok[f"y2ch{cb}"] = (s_pe, pv)
                    tok[f"ych{cb}"] = (s_pe, pv)
                    em.wait("scalar", s_pe, pv)
                    vsc = (EPS * HD) / (2.0 ** -10) if side == "q" else 1.0
                    av = em.op("scalar", lambda e, cb=cb, hl=hl, tc=tc, vrow=vrow, vsc=vsc: e.activation(
                        vrow[0:1, hl * T + tc * 512: hl * T + tc * 512 + 512],
                        ps_rb[cb][0:1, :], AF.Copy, scale=vsc), s_act)
                    tok[f"pb{cb + 2}"] = (s_act, av)

                if g == 0 and _job_i == 0:
                    for hf in (0, 1):
                        emit_wlo_half(0, 0, hf)
                        emit_wlo_half(0, 1, hf)
                for tc in range(NTC):
                    pv_last = emit_ymm(tc)
                    if tc >= 1:
                        emit_statsA(tc - 1)
                        if tc in (1, 3) and _job_i + 2 < len(jobs):
                            emit_wlo_half(g, _job_i + 2, tc // 2)
                    if tc >= 2:
                        emit_statsB(tc - 2)
                emit_statsA(NTC - 1)
                emit_statsB(NTC - 2)
                emit_statsB(NTC - 1)

                em.wait("tensor", *gtok(f"wlo_rdy{_job_i % 3}"))
                em.wait("tensor", *gtok("pb6"))
                pv = None
                for dc in range(NDC):
                    em.op("tensor", lambda e, wb=wb, dc=dc: e.matmul(
                        ps_oz[1][:, 0:128], wslab[wb][:, dc * 128:(dc + 1) * 128],
                        xnloT[:, dc * 128:(dc + 1) * 128],
                        start=(dc == 0), stop=False))
                for dc in range(NDC):
                    pv = em.op("tensor", lambda e, w3=_job_i % 3, dc=dc: e.matmul(
                        ps_oz[1][:, 0:128], wlo[w3][:, dc * 128:(dc + 1) * 128],
                        xnT[:, dc * T: dc * T + 128],
                        start=False, stop=(dc == NDC - 1)),
                        s_pe if dc == NDC - 1 else None)
                em.wait("scalar", s_pe, pv)
                av = em.op("scalar", lambda e: e.activation(
                    corrf[:], ps_oz[1][:, 0:128], AF.Copy, scale=gam[:]), s_act)
                tok["pb6"] = (s_act, av)
                em.wait("vector", s_act, av)
                dv = em.op("vector", lambda e: e.tensor_add(
                    corrf[:], ych0f[:], corrf[:]), s_dve)
                em.wait("vector", s_dve, dv)
                y0f = yk0f if side == "k" else yq0f
                dv = em.op("vector", lambda e, hl=hl, y0f=y0f: e.scalar_tensor_tensor(
                    y0f[:, hl * 128:(hl + 1) * 128], corrf[:],
                    betc[:], mu0f[:], ALU.add, ALU.subtract), s_dve)
                tok["ych0f"] = (s_dve, dv)
                tok[f"wslab{wb}"] = (s_pe, pv)
                tok[f"wlo{_job_i % 3}"] = (s_pe, pv)
        vark_done = s_act.n
        for hl in range(GH):
            em.wait("tensor", s_act, vark_done)
            em.wait("tensor", *gtok("pb5"))
            pv = None
            for tt in range(NTT):
                pv = em.op("tensor", lambda e, hl=hl, tt=tt: e.matmul(
                    ps_oz[0][:, tt:tt + 1],
                    vark_row[0:1, hl * T + tt * 128: hl * T + tt * 128 + 128],
                    one11[:], start=True, stop=True), s_pe)
            em.wait("vector", s_pe, pv)
            dv = em.op("vector", lambda e: e.tensor_scalar_add(
                invk_tmp[:], ps_oz[0][:, 0:16], EPS), s_dve)
            tok["pb5"] = (s_dve, dv)
            em.wait("vector", s_dve, dv)
            dv = em.op("vector", lambda e, hl=hl: e.reciprocal(
                invk_cols[:, hl * 16: hl * 16 + 16], invk_tmp[:]), s_dve)
        invk_done = s_dve.n
        wv_view = wv_d[g].rearrange("p (dc m) -> p dc m", dc=NDC)
        for eh in range(2):
            em.wait("gpsimd", *gtok("vslab"))
            em.op("gpsimd", lambda e, wvv=wv_view, eh=eh: e.dma_start(
                out=vslab[:].rearrange("p (dc m) -> p dc m", dc=NDC),
                in_=wvv[:, :, eh * 256:(eh + 1) * 256]), s_vsl, 16)
            pv = None
            for tt in range(NTT):
                vb = tt % 2
                em.wait("tensor", s_vsl, 0)
                em.wait("tensor", *gtok(f"pb{vb}"))
                for dc in range(NDC):
                    pv = em.op("tensor", lambda e, vb=vb, dc=dc, tt=tt: e.matmul(
                        ps_y[vb][:, 0:256],
                        xnT[:, dc * T + tt * 128: dc * T + tt * 128 + 128],
                        vslab[:, dc * 256:(dc + 1) * 256],
                        start=(dc == 0), stop=(dc == NDC - 1)),
                        s_pe if dc == NDC - 1 else None)
                if tt == NTT - 1:
                    tok["vslab"] = (s_pe, pv)
                em.wait("vector", s_pe, pv)
                em.wait("vector", s_dve, invk_done)
                dv = None
                for hh in range(2):
                    hl = eh * 2 + hh
                    off = tt * (GH * 129) + hl * 129
                    dv = em.op("vector", lambda e, vb=vb, off=off, hh=hh, hl=hl, tt=tt: e.tensor_scalar_mul(
                        vaug[:, off: off + 128], ps_y[vb][:, hh * 128:(hh + 1) * 128],
                        invk_cols[:, hl * 16 + tt: hl * 16 + tt + 1]), s_dve)
                    dv = em.op("vector", lambda e, off=off, hl=hl, tt=tt: e.tensor_copy(
                        vaug[:, off + 128: off + 129],
                        invk_cols[:, hl * 16 + tt: hl * 16 + tt + 1]), s_dve)
                tok[f"pb{vb}"] = (s_dve, dv)

        qkv_done = snap()
        if dbg and g == 0:
            for e_ in ENGINES:
                em.waitsnap(e_, qkv_done)
            for nm, src in (("ybq", ybq), ("ybk", ybk), ("vaug", vaug),
                            ("invk", invk_cols), ("varq", varq_row)):
                em.op("sync", lambda e, nm=nm, src=src: e.dma_start(
                    out=dbg_d[nm], in_=src[:]), s_dbg2, 16)
        if stage == "qkv" and g == 0:
            break

        for e_ in ENGINES:
            em.waitsnap(e_, qkv_done)
        ozbank = {0: 5, 1: 6, 2: 7, 3: 0}
        RBL = (2, 3, 4, 1)
        oz_stop_pv = {}
        post_state = {}
        pv_grp = None

        for hl in range(GH):
            for c in range(NTC):
                nj = 4 * c + 4

                def emit_dots(j, hl=hl, c=c):
                    bk = RBL[rbc[0] % 4]
                    rbc[0] += 1
                    r = j % 3
                    em.wait("tensor", *gtok(f"pb{bk}"))
                    pv = em.op("tensor", lambda e, bk=bk, hl=hl, j=j, c=c: e.matmul(
                        pbk[bk][:], ybk[:, hl * T + j * 128: hl * T + j * 128 + 128],
                        ybq[:, hl * T + c * 512: hl * T + c * 512 + 512],
                        start=True, stop=True), s_pe)
                    jl = j - 4 * c
                    if c == 0 and j == 0:
                        em.wait("scalar", s_pe, pv)
                        em.wait("scalar", *gtok("aSfree0"))
                        av = em.op("scalar", lambda e, bk=bk: e.activation(
                            attnS[0][:, 128:512], pbk[bk][:, 128:512],
                            AF.Square), s_act)
                        tok[f"pb{bk}"] = (s_act, av)
                        tok["aS0"] = (s_act, av)
                        em.wait("tensor", *gtok(f"pb{bk}"))
                        pv = em.op("tensor", lambda e, bk=bk, hl=hl: e.matmul(
                            pbk[bk][:, 0:128], yk0f[:, hl * 128:(hl + 1) * 128],
                            yq0f[:, hl * 128:(hl + 1) * 128],
                            start=True, stop=True), s_pe)
                        em.wait("scalar", s_pe, pv)
                        em.wait("scalar", *gtok("attn0b"))
                        av = em.op("scalar", lambda e, bk=bk: e.activation(
                            attn0b[:], pbk[bk][:, 0:128], AF.Square), s_act)
                        em.wait("vector", s_act, av)
                        dvm = em.op("vector", lambda e: e.tensor_mul(
                            attn0b[:], attn0b[:], dmask[:]), s_dve)
                        tok[f"pb{bk}"] = (s_act, av)
                        tok["attn0f_rdy"] = (s_dve, dvm)
                    elif jl >= 0:
                        em.wait("scalar", s_pe, pv)
                        em.wait("scalar", *gtok(f"aSfree{r}"))
                        av = em.op("scalar", lambda e, bk=bk, r=r: e.activation(
                            attnS[r][:], pbk[bk][:], AF.Square), s_act)
                        em.wait("vector", s_act, av)
                        dvm = em.op("vector", lambda e, r=r, jl=jl: e.tensor_mul(
                            attnS[r][:, jl * 128: jl * 128 + 128],
                            attnS[r][:, jl * 128: jl * 128 + 128],
                            dmask[:]), s_dve)
                        tok[f"pb{bk}"] = (s_act, av)
                        tok[f"aS{r}"] = (s_dve, dvm)
                    else:
                        em.wait("scalar", s_pe, pv)
                        em.wait("scalar", *gtok(f"aSfree{r}"))
                        av = em.op("scalar", lambda e, bk=bk, r=r: e.activation(
                            attnS[r][:], pbk[bk][:], AF.Square), s_act)
                        tok[f"pb{bk}"] = (s_act, av)
                        tok[f"aS{r}"] = (s_act, av)

                def emit_ozb(j, hl=hl, c=c):
                    r = j % 3
                    lo = max(0, j - 4 * c)
                    em.wait("tensor", *gtok(f"aS{j % 3}" if not (c == 0 and j == 0)
                                            else "attn0f_rdy"))
                    if c == 0 and j == 0:
                        em.wait("tensor", *gtok("aS0"))
                    if j == 0:
                        for il in range(4):
                            em.wait("tensor", *gtok(f"pb{ozbank[il]}"))
                    pv = None
                    for il in range(lo, 4):
                        if c == 0 and j == 0 and il == 0:
                            pv = em.op("tensor", lambda e, hl=hl: e.matmul(
                                ps_oz[0][:, 0:129], attn0b[:],
                                vaug[:, hl * 129:(hl + 1) * 129],
                                start=True, stop=False, skip_group_check=True),
                                s_pe)
                            continue
                        pv = em.op("tensor", lambda e, il=il, r=r, j=j, hl=hl: e.matmul(
                            ps_oz[il][:, 0:129],
                            attnS[r][:, il * 128: il * 128 + 128],
                            vaug[:, j * (GH * 129) + hl * 129:
                                 j * (GH * 129) + hl * 129 + 129],
                            start=(j == 0), stop=False, skip_group_check=True),
                            s_pe)
                    if c == 0 and j == 0:
                        tok["attn0b"] = (s_pe, pv)
                    tok[f"aSfree{r}"] = (s_pe, pv)
                    return pv

                def emit_stop(il, hl=hl, c=c):
                    i = 4 * c + il
                    pv = em.op("tensor", lambda e, il=il, hl=hl, i=i: e.matmul(
                        ps_oz[il][:, 0:129],
                        varq_row[0:1, hl * T + i * 128: hl * T + i * 128 + 128],
                        ecol[:], start=False, stop=True, skip_group_check=True),
                        s_pe)
                    oz_stop_pv[il] = pv
                    return pv

                def emit_post_dve(il, hl=hl, c=c):
                    pv = oz_stop_pv[il]
                    pr = il
                    em.wait("vector", s_pe, pv)
                    em.wait("vector", *gtok("zt"))
                    dv = em.op("vector", lambda e, il=il: e.tensor_scalar_add(
                        zt[:], ps_oz[il][:, 128:129], EPS * EPS * HD), s_dve)
                    em.wait("vector", s_dve, dv)
                    dv = em.op("vector", lambda e: e.reciprocal(rec[:], zt[:]), s_dve)
                    tok["zt"] = (s_dve, dv)
                    em.wait("vector", s_dve, dv)
                    em.wait("vector", *gtok(f"onat{pr}"))
                    dv = em.op("vector", lambda e, il=il, pr=pr: e.tensor_scalar_mul(
                        onat[pr][:], ps_oz[il][:, 0:128], rec[:]), s_dve)
                    tok[f"pb{ozbank[il]}"] = (s_dve, dv)
                    post_state[(hl, c, il)] = dv

                def emit_post_tr(il, hl=hl, c=c, g=g):
                    i = 4 * c + il
                    pr = il
                    bk = RBL[rbc[0] % 4]
                    rbc[0] += 1
                    em.wait("tensor", s_dve, post_state[(hl, c, il)])
                    em.wait("tensor", *gtok(f"pb{bk}"))
                    pv = em.op("tensor", lambda e, pr=pr, bk=bk: e.matmul(
                        pbk[bk][:, 0:128], onat[pr][:], ident[:],
                        start=True, stop=True), s_pe)
                    tok[f"onat{pr}"] = (s_pe, pv)
                    em.wait("scalar", s_pe, pv)
                    if c == 0 and il == 0:
                        em.wait("scalar", *gtok("oTgS"))
                        em.wait("scalar", *gtok("oTgS2"))
                    av = em.op("scalar", lambda e, i=i, bk=bk: e.activation(
                        oTg[:, i * 128: i * 128 + 128],
                        pbk[bk][:, 0:128], AF.Copy), s_act)
                    tok[f"pb{bk}"] = (s_act, av)

                for j in range(nj):
                    if j == 2 and pending_post:
                        for fn in pending_post:
                            fn()
                        pending_post[:] = []
                    emit_dots(j)
                    if j >= 2:
                        emit_ozb(j - 2)
                emit_ozb(nj - 2)
                emit_ozb(nj - 1)
                for il in range(4):
                    pv_grp = emit_stop(il)
                for il in range(4):
                    emit_post_dve(il)
                pending_post[:] = [
                    (lambda il=il, f=emit_post_tr: f(il)) for il in range(4)]
                tok["grp_free"] = (s_pe, pv_grp)
            for fn in pending_post:
                fn()
            pending_post[:] = []
            em.wait("sync", s_act, s_act.n)
            em.op("sync", lambda e, g=g, hl=hl: e.dma_start(
                out=oT_scr[(g * GH + hl) * 128:(g * GH + hl) * 128 + 128, :],
                in_=oTg[:]), s_osp2[0], 16)
            tok["oTgS"] = (s_osp2[0], 0)
            if dbg and g == 0 and hl == 0:
                em.op("sync", lambda e: e.dma_start(
                    out=dbg_d["oTg"], in_=oTg[:]), s_dbg2, 16)
                tok["oTgS2"] = (s_dbg2, 0)
        if g < NG - 1:
            emit_wload_g(g + 1, 0)
            emit_wload_g(g + 1, 1)
            for hf in (0, 1):
                emit_wlo_half(g + 1, 0, hf)
                emit_wlo_half(g + 1, 1, hf)
            for e_ in ("tensor", "vector", "scalar"):
                em.wait(e_, *gtok("grp_free"))
        else:
            barrier()
        if stage == "attn" and g == 0:
            break

    if stage in ("qkv", "attn"):
        em.wait("sync", s_dbg2, 0)
        em.wait("sync", s_osp2[0], 0)
        em.wait("sync", s_osp2[1], 0)
        blk = nc.Block().__enter__()
        em.emit(blk)
        blk.__exit__(None, None, None)
        gc.close()
        ctx.close()
        return nc
    gc.close()

    woc = ExitStack()
    wsb = lambda name, shape, dt: woc.enter_context(nc.sbuf_tensor(name, shape, dt))
    wps = lambda name, shape: woc.enter_context(nc.psum_tensor(name, shape, F32))
    wo_sb = wsb("wo_sb", [128, 16 * 2048], BF16)
    oTsl = [wsb(f"oTsl_{i}", [128, 16 * 512], BF16) for i in range(2)]
    accst = [wsb(f"accst_{i}", [128, 2048], F32) for i in range(2)]
    ps_ao = [wps(f"ps_ao_{i}", [128, 512]) for i in range(4)]
    s_wo = mksem("s_wo", True)
    s_otl = [mksem(f"s_otl{i}", True) for i in range(2)]
    s_xr = [mksem(f"s_xr{i}", True) for i in range(2)]
    s_acst = [mksem(f"s_acst{i}", True) for i in range(2)]

    for ec in range(16):
        em.op("gpsimd", lambda e, ec=ec: e.dma_start(
            out=wo_sb[:, ec * 2048:(ec + 1) * 2048],
            in_=wo_d[ec * 128:(ec + 1) * 128, :]), s_wo, 16)
    oT_view = oT_scr.rearrange("(ec p) t -> p ec t", p=128)
    for tc in range(NTC):
        ob = tc % 2
        em.wait("sync", *gtok(f"oTsl{ob}"))
        for hf in range(2):
            em.op("sync", lambda e, ob=ob, tc=tc, hf=hf: e.dma_start(
                out=oTsl[ob][:].rearrange("p (ec t) -> p ec t", ec=16)
                [:, hf * 8:(hf + 1) * 8, :],
                in_=oT_view[:, hf * 8:(hf + 1) * 8, tc * 512:(tc + 1) * 512]),
                s_otl[ob], 16)
        pv_last = None
        for ttl in range(4):
            tt = tc * 4 + ttl
            xb = tt % 2
            em.wait("sync", *gtok(f"accst{xb}"))
            em.wait("sync", *gtok(f"xres{xb}"))
            em.op("sync", lambda e, xb=xb, tt=tt: e.dma_start(
                out=accst[xb][:], in_=x_d[tt * 128:(tt + 1) * 128, :]), s_xr[xb], 16)
            for ddc in range(4):
                pb = ddc % 4
                em.wait("tensor", s_wo, 0)
                em.wait("tensor", s_otl[ob], 0)
                em.wait("tensor", *gtok(f"ps_ao{pb}"))
                pv = None
                for ec in range(16):
                    pv = em.op("tensor", lambda e, pb=pb, ob=ob, ec=ec, ttl=ttl, ddc=ddc: e.matmul(
                        ps_ao[pb][:], oTsl[ob][:, ec * 512 + ttl * 128: ec * 512 + ttl * 128 + 128],
                        wo_sb[:, ec * 2048 + ddc * 512: ec * 2048 + ddc * 512 + 512],
                        start=(ec == 0), stop=(ec == 15)), s_pe if ec == 15 else None)
                pv_last = pv
                em.wait("vector", s_pe, pv)
                em.wait("vector", s_xr[xb], 0)
                dv = em.op("vector", lambda e, pb=pb, xb=xb, ddc=ddc: e.tensor_add(
                    accst[xb][:, ddc * 512:(ddc + 1) * 512], ps_ao[pb][:],
                    accst[xb][:, ddc * 512:(ddc + 1) * 512]), s_dve)
                tok[f"ps_ao{pb}"] = (s_dve, dv)
                if ddc == 3:
                    tok[f"xres{xb}"] = (s_dve, dv)
            em.wait("sync", s_dve, s_dve.n)
            em.op("sync", lambda e, xb=xb, tt=tt: e.dma_start(
                out=acc_scr[tt * 128:(tt + 1) * 128, :], in_=accst[xb][:]), s_acst[xb], 16)
            tok[f"accst{xb}"] = (s_acst[xb], 0)
        tok[f"oTsl{ob}"] = (s_pe, pv_last)
    barrier()
    woc.close()
    if stage == "wo":
        blk = nc.Block().__enter__()
        em.emit(blk)
        blk.__exit__(None, None, None)
        ctx.close()
        return nc

    fc = ExitStack()
    fsb = lambda name, shape, dt: fc.enter_context(nc.sbuf_tensor(name, shape, dt))
    fps = lambda name, shape: fc.enter_context(nc.psum_tensor(name, shape, F32))
    w1sl = [fsb(f"w1sl_{i}", [128, 2048], BF16) for i in range(2)]
    w3sl = [fsb(f"w3sl_{i}", [128, 2048], BF16) for i in range(2)]
    silust = [fsb(f"silust_{i}", [128, 512], F32) for i in range(2)]
    hidst = [fsb(f"hidst_{i}", [128, 512], BF16) for i in range(2)]
    ust = [fsb(f"ust_{i}", [128, 512], BF16) for i in range(2)]
    ps_g = [fps(f"ps_g_{i}", [128, 512]) for i in range(2)]
    ps_u3 = [fps(f"ps_u3_{i}", [128, 512]) for i in range(2)]
    ps_uh = [fps(f"ps_uh_{i}", [128, 512]) for i in range(2)]
    s_wA = [mksem(f"s_wA{i}", True) for i in range(2)]
    s_ust = [mksem(f"s_ust{i}", True) for i in range(2)]

    def emit_wAload(blk_i):
        bb = blk_i % 2
        em.wait("gpsimd", *gtok(f"w1sl{bb}"))
        em.op("gpsimd", lambda e, bb=bb, blk_i=blk_i: e.dma_start(
            out=w1sl[bb][:], in_=w1_d[blk_i]), s_wA[bb], 16)
        em.op("gpsimd", lambda e, bb=bb, blk_i=blk_i: e.dma_start(
            out=w3sl[bb][:], in_=w3_d[blk_i]), s_wA[bb], 16)

    emit_wAload(0)
    up_pv = {}

    def emit_ffn_up(st):
        blk_i, tc = divmod(st, NTC)
        bb = blk_i % 2
        cb = tc % 2
        em.wait("tensor", s_wA[bb], 0)
        em.wait("tensor", *gtok(f"ps_g{cb}"))
        for dc in range(NDC):
            em.op("tensor", lambda e, cb=cb, bb=bb, dc=dc, tc=tc: e.matmul(
                ps_g[cb][:], w1sl[bb][:, dc * 128:(dc + 1) * 128],
                xnT[:, dc * T + tc * 512: dc * T + tc * 512 + 512],
                start=(dc == 0), stop=(dc == NDC - 1)),
                s_pe if dc == NDC - 1 else None)
        em.wait("tensor", *gtok(f"ps_u3{cb}"))
        pv3 = None
        for dc in range(NDC):
            pv3 = em.op("tensor", lambda e, cb=cb, bb=bb, dc=dc, tc=tc: e.matmul(
                ps_u3[cb][:], w3sl[bb][:, dc * 128:(dc + 1) * 128],
                xnT[:, dc * T + tc * 512: dc * T + tc * 512 + 512],
                start=(dc == 0), stop=(dc == NDC - 1)),
                s_pe if dc == NDC - 1 else None)
        up_pv[st] = pv3
        if tc == NTC - 1:
            tok[f"w1sl{bb}"] = (s_pe, pv3)

    def emit_ffn_epi(st):
        blk_i, tc = divmod(st, NTC)
        cb = tc % 2
        pv3 = up_pv[st]
        em.wait("scalar", s_pe, pv3)
        em.wait("scalar", *gtok(f"silust{cb}"))
        av = em.op("scalar", lambda e, cb=cb: e.activation(
            silust[cb][:], ps_g[cb][:], AF.Silu), s_act)
        tok[f"ps_g{cb}"] = (s_act, av)
        em.wait("vector", s_act, av)
        em.wait("vector", *gtok(f"hidst{cb}"))
        dv = em.op("vector", lambda e, cb=cb: e.tensor_mul(
            hidst[cb][:], silust[cb][:], ps_u3[cb][:]), s_dve)
        tok[f"ps_u3{cb}"] = (s_dve, dv)
        tok[f"silust{cb}"] = (s_dve, dv)
        em.wait("tensor", s_dve, dv)
        em.wait("tensor", *gtok(f"ps_uh{cb}"))
        pv = em.op("tensor", lambda e, cb=cb: e.matmul(
            ps_uh[cb][:], h128[:], hidst[cb][:], start=True, stop=True), s_pe)
        tok[f"hidst{cb}"] = (s_pe, pv)
        em.wait("scalar", s_pe, pv)
        em.wait("scalar", *gtok(f"ust{cb}"))
        av = em.op("scalar", lambda e, cb=cb: e.activation(
            ust[cb][:], ps_uh[cb][:], AF.Copy), s_act)
        tok[f"ps_uh{cb}"] = (s_act, av)
        em.wait("sync", s_act, av)
        em.op("sync", lambda e, cb=cb, blk_i=blk_i, tc=tc: e.dma_start(
            out=u_scr[blk_i * 128:(blk_i + 1) * 128, tc * 512:(tc + 1) * 512],
            in_=ust[cb][:]), s_ust[cb], 16)
        tok[f"ust{cb}"] = (s_ust[cb], 0)

    for st in range(NHB * NTC):
        blk_i, tc = divmod(st, NTC)
        if tc == 0 and blk_i + 1 < NHB:
            emit_wAload(blk_i + 1)
        emit_ffn_up(st)
        if st >= 1:
            emit_ffn_epi(st - 1)
    emit_ffn_epi(NHB * NTC - 1)
    barrier()
    fc.close()
    if stage == "ffna":
        blk = nc.Block().__enter__()
        em.emit(blk)
        blk.__exit__(None, None, None)
        ctx.close()
        return nc

    bc = ExitStack()
    bsb = lambda name, shape, dt: bc.enter_context(nc.sbuf_tensor(name, shape, dt))
    bps = lambda name, shape: bc.enter_context(nc.psum_tensor(name, shape, F32))
    u_sl = bsb("u_sl", [128, NHB * 512], BF16)
    acc_sl = bsb("acc_sl", [128, 4 * 1024], F32)
    w2sl = [bsb(f"w2sl_{i}", [128, 2048], BF16) for i in range(3)]
    outst = [bsb(f"outst_{i}", [128, 1024], F32) for i in range(4)]
    ps_o = [bps(f"ps_o_{i}", [128, 512]) for i in range(8)]
    s_usl = mksem("s_usl", True)
    s_accl = mksem("s_accl", True)
    s_w2s = [mksem(f"s_w2s{i}", True) for i in range(3)]
    s_uslB = mksem("s_uslB", True)
    s_outB = mksem("s_outB", True)
    u_view = u_scr.rearrange("(hb p) t -> p hb t", p=128)
    acc_view = acc_scr.rearrange("(tcg ttl p) d -> tcg p ttl d", ttl=4, p=128)

    def emit_uload(tc, half):
        em.wait("sync", *gtok("u_slA" if half == 0 else "u_slB"))
        for part in range(4 * half, 4 * half + 4):
            em.op("sync", lambda e, tc=tc, part=part: e.dma_start(
                out=u_sl[:].rearrange("p (hb t) -> p hb t", hb=NHB)
                [:, part * 8:(part + 1) * 8, :],
                in_=u_view[:, part * 8:(part + 1) * 8, tc * 512:(tc + 1) * 512]),
                s_usl if half == 0 else s_uslB, 16)

    emit_uload(0, 0)
    emit_uload(0, 1)
    for tc in range(NTC):
        dv_last = None
        NQ = NHB // 2

        def emit_w2sload(qd, ddh):
            wb = (ddh * NQ + qd) % 3
            em.wait("scalar", *gtok(f"w2sl{wb}"))
            em.op("scalar", lambda e, wb=wb, qd=qd, ddh=ddh: e.dma_start(
                out=w2sl[wb][:].rearrange("p (b d) -> p b d", b=2),
                in_=w2pp_scr.rearrange("(b p) d -> p b d", p=128)
                [:, qd * 2:(qd + 1) * 2, ddh * 1024:(ddh + 1) * 1024]),
                s_w2s[wb], 16)

        for ddh in range(2):
            em.wait("sync", *gtok("acc_sl"))
            em.op("sync", lambda e, tc=tc, ddh=ddh: e.dma_start(
                out=acc_sl[:].rearrange("p (ttl d) -> p ttl d", ttl=4),
                in_=acc_view[tc][:, :, ddh * 1024:(ddh + 1) * 1024]), s_accl, 16)
            pv_fin = None
            if ddh == 0:
                emit_w2sload(0, 0)
                emit_w2sload(1, 0)
            for qd in range(NQ):
                wb = (ddh * NQ + qd) % 3
                if qd + 2 < NQ:
                    emit_w2sload(qd + 2, ddh)
                elif ddh == 0:
                    emit_w2sload(qd + 2 - NQ, 1)
                em.wait("tensor", s_w2s[wb], 0)
                if qd == 0:
                    em.wait("tensor", s_usl, 0)
                    for k in range(8):
                        em.wait("tensor", *gtok(f"ps_o{k}"))
                elif qd == NQ // 2:
                    em.wait("tensor", s_uslB, 0)
                pv = None
                for hq in range(2):
                    hb = qd * 2 + hq
                    for ttl in range(4):
                        for dd5 in range(2):
                            pv = em.op("tensor", lambda e, ttl=ttl, dd5=dd5, wb=wb, hb=hb, hq=hq: e.matmul(
                                ps_o[ttl * 2 + dd5][:],
                                u_sl[:, hb * 512 + ttl * 128: hb * 512 + ttl * 128 + 128],
                                w2sl[wb][:, hq * 1024 + dd5 * 512: hq * 1024 + dd5 * 512 + 512],
                                start=(hb == 0), stop=(hb == NHB - 1),
                                skip_group_check=True), s_pe)
                tok[f"w2sl{wb}"] = (s_pe, pv)
                pv_fin = pv
                if ddh == 1 and qd == NQ // 2 - 1:
                    tok["u_slA"] = (s_pe, pv)
                    if tc + 1 < NTC:
                        emit_uload(tc + 1, 0)
            if ddh == 1:
                tok["u_slB"] = (s_pe, pv_fin)
                if tc + 1 < NTC:
                    emit_uload(tc + 1, 1)
            for ttl in range(4):
                ttl_dv = None
                for dd5 in range(2):
                    k = ttl * 2 + dd5
                    em.wait("vector", s_pe, pv_fin)
                    em.wait("vector", s_accl, 0)
                    if dd5 == 0:
                        em.wait("vector", *gtok(f"outst{ttl}"))
                    dv = em.op("vector", lambda e, k=k, ttl=ttl, dd5=dd5: e.tensor_add(
                        outst[ttl][:, dd5 * 512: dd5 * 512 + 512],
                        ps_o[k][:],
                        acc_sl[:, ttl * 1024 + dd5 * 512:
                               ttl * 1024 + dd5 * 512 + 512]), s_dve)
                    tok[f"ps_o{k}"] = (s_dve, dv)
                    ttl_dv = dv
                    dv_last = dv
                tt = tc * 4 + ttl
                em.wait("sync", s_dve, ttl_dv)
                em.op("sync", lambda e, ttl=ttl, tt=tt, ddh=ddh: e.dma_start(
                    out=out_d[tt * 128:(tt + 1) * 128, ddh * 1024:(ddh + 1) * 1024],
                    in_=outst[ttl][:]), s_outB, 16)
                tok[f"outst{ttl}"] = (s_outB, 0)
            tok["acc_sl"] = (s_dve, dv_last)

    em.wait("sync", s_outB, 0)
    if dbg:
        em.wait("sync", s_dbg, 0)
        em.wait("sync", s_dbg2, 0)
    blk = nc.Block().__enter__()
    em.emit(blk)
    blk.__exit__(None, None, None)
    bc.close()
    ctx.close()
    return nc


def hmat(n):
    h = np.array([[1.0]], dtype=np.float64)
    while h.shape[0] < n:
        h = np.block([[h, h], [h, -h]])
    return h


def prep_inputs(inputs):
    x = np.ascontiguousarray(np.asarray(inputs["L0_attn_hidden_states"]))
    Wq = np.asarray(inputs["L0_attn_Wq"]); Wk = np.asarray(inputs["L0_attn_Wk"])
    Wv = np.asarray(inputs["L0_attn_Wv"]); Wo = np.asarray(inputs["L0_attn_Wo"])
    gamma = np.asarray(inputs["L0_attn_gamma"]); beta = np.asarray(inputs["L0_attn_beta"])
    W1 = np.asarray(inputs["L0_ffn_W1"]); W2 = np.asarray(inputs["L0_ffn_W2"])
    W3 = np.asarray(inputs["L0_ffn_W3"])

    bf = ml_dtypes.bfloat16
    blk16 = lambda W: np.ascontiguousarray(
        W.reshape(16, 128, 16, 128).transpose(0, 3, 2, 1)).reshape(16, 128, 2048)
    wq = blk16(Wq)
    wk = blk16(Wk)
    wv = np.ascontiguousarray(
        Wv.reshape(4, 512, 16, 128).transpose(0, 3, 2, 1)).reshape(4, 128, 8192)
    wo = np.ascontiguousarray(Wo.T)
    blk64 = lambda W: np.ascontiguousarray(
        W.reshape(64, 128, 16, 128).transpose(0, 3, 2, 1)).reshape(64, 128, 2048)
    w1 = blk64(W1)
    w3 = blk64(W3)
    w2T = np.ascontiguousarray(W2.T)
    perm = np.arange(NH)
    perm = (perm % 64) * 128 + (perm // 64)
    w2p = np.ascontiguousarray(w2T[perm])

    consts = {
        "ident": np.eye(128).astype(bf),
        "h128": hmat(128).astype(bf),
        "h64d": np.kron(np.eye(2), hmat(64)).astype(bf),
        "oneD": np.full((128, 1), 1.0 / 128.0).astype(bf),
        "onedm": np.full((128, 128), 1.0 / 128.0).astype(bf),
        "dmask": np.concatenate(
            [np.concatenate([np.triu(np.ones((128, 128))) if il == jl else
                             np.ones((128, 128)) for il in range(4)], axis=1)
             for jl in range(4)], axis=1).astype(bf),
        "onesr": np.ones((1, 128)).astype(bf),
        "one11": np.ones((1, 1)).astype(bf),
        "ecol": np.concatenate([np.zeros(128), [2.0 ** -10]]).reshape(1, 129).astype(bf),
        "gamma": np.ascontiguousarray(np.asarray(gamma, np.float32).reshape(128, 1)),
        "beta": np.ascontiguousarray(np.asarray(beta, np.float32).reshape(128, 1)),
    }
    shared = dict(wq=wq, wk=wk, wv=wv, wo=wo, w1=w1, w3=w3, w2p=w2p, **consts)
    in_maps = []
    for b in range(x.shape[0]):
        m = dict(shared)
        m["x"] = np.ascontiguousarray(x[b])
        in_maps.append(m)
    return in_maps


from concourse.bass_utils import run_bass_kernel_spmd


def kernel(**inputs):
    nc = build_program(stage="full", dbg=False)
    in_maps = prep_inputs(inputs)
    res = run_bass_kernel_spmd(nc, in_maps, core_ids=list(range(8)))
    return np.stack([np.asarray(r["out"], dtype=np.float32) for r in res.results],
                    axis=0)
